# version 12
# baseline (speedup 1.0000x reference)
import contextlib
import numpy as np
import ml_dtypes
import concourse.bass as bass
import concourse.mybir as mybir
from concourse.bass_utils import run_bass_kernel_spmd

F32 = mybir.dt.float32
BF16 = mybir.dt.bfloat16
I32 = mybir.dt.int32
ALU = mybir.AluOpType
AF = mybir.ActivationFunctionType
AX = mybir.AxisListType

NCORES = 8
BPC = 2
L = 4096
H = 1024
P = 256
TB = 512
NBLK = L // TB
NSUB = TB // 128
NHT = H // 128
FF = 4096
NFT = FF // 128
EPS = 1e-5
INV_H = 1.0 / H
TWO_PI_HI = 6.28125
TWO_PI_LO = 0.0019353071795864769
INV_2PI = float(1.0 / (2.0 * np.pi))
RSQRT_MAGIC = 0x5f3759df


class Sem:
    def __init__(self, name):
        self.name = name
        self.count = 0
        self.handle = None


class Plan:
    def __init__(self):
        self.steps = []
        self.sems = []

    def sem(self, name):
        s = Sem(name)
        self.sems.append(s)
        return s

    def step(self, engine, fn, waits=(), incs=()):
        ws = [(s, v) for (s, v) in waits if v > 0]
        incs = list(incs)
        for s, n in incs:
            s.count += n
        self.steps.append((engine, fn, ws, incs))


def build_program():
    nc = bass.Bass("TRN2", target_bir_lowering=False, debug=False)

    def din(name, shape, dt=F32):
        return nc.dram_tensor(name, shape, dt, kind="ExternalInput").ap()

    x_in = din("L0_attn_x", [BPC, L, H])
    prm = {}
    for d in ("fwd", "bwd"):
        for nm, shp in [("log_real", [P]), ("imag", [P]), ("B_real", [P, H]),
                        ("B_imag", [P, H]), ("C_real", [H, P]), ("C_imag", [H, P]),
                        ("D", [H]), ("log_Delta", [P])]:
            prm[f"{d}_{nm}"] = din(f"L0_attn_{d}_{nm}", shp)
    ln_g = din("L0_ffn_ln_gamma", [H])
    ln_b = din("L0_ffn_ln_beta", [H])
    fc1_q = din("L0_ffn_fc1_q", [H, FF], I32)
    fc1_sc = din("L0_ffn_fc1_scales", [H // 128, FF])
    fc1_b = din("L0_ffn_fc1_bias", [FF])
    fc2_q = din("L0_ffn_fc2_q", [FF, H], I32)
    fc2_sc = din("L0_ffn_fc2_scales", [FF // 128, H])
    fc2_b = din("L0_ffn_fc2_bias", [H])
    idf_in = din("c_idf", [128, 128])
    idb_in = din("c_idb", [128, 128], BF16)

    y_out = nc.dram_tensor("out", [BPC, L, H], F32, kind="ExternalOutput").ap()

    w1s = nc.dram_tensor("w1s", [NFT, 128, NHT, 128], BF16, kind="Internal").ap()
    w2s = nc.dram_tensor("w2s", [NFT, 128, H], BF16, kind="Internal").ap()
    xsb = nc.dram_tensor("xsb", [BPC, 2, 2, 128, L], BF16, kind="Internal").ap()
    xnt_s = nc.dram_tensor("xnt_s", [BPC, NBLK, 128, NHT, TB], BF16, kind="Internal").ap()

    es = contextlib.ExitStack()

    def sb(name, shape, dt=F32):
        return es.enter_context(nc.sbuf_tensor(name, shape, dt))

    xbuf = sb("xbuf", [128, NSUB, H])
    xnT = sb("xnT", [128, NHT, TB], BF16)
    hT = sb("hT", [128, NHT, TB], BF16)
    sbb = sb("sbb", [128, 4, TB], BF16)
    ssf = sb("ssf", [128, 4, TB], BF16)
    wpl = sb("wpl", [128, 4, TB])
    fpl = sb("fpl", [128, 4, TB])
    bpr = sb("bpr", [128, 4, TB])
    tmpa = sb("tmpa", [128, TB])
    tmpb = sb("tmpb", [128, TB])
    gtmp = sb("gtmp", [128, 2, TB])
    hbuf = sb("hbuf", [128, 2, H])
    ybuf = sb("ybuf", [128, NFT, TB], BF16)
    w1b = sb("w1b", [128, 2, 2, NHT, 128], BF16)
    w2b = sb("w2b", [128, 2, 2, 512], BF16)
    obuf = sb("obuf", [128, 4, 512])
    rbase = sb("rbase", [128, 2, 2, 2, TB])
    btst = sb("btst", [128, 2, NHT, 2, P], BF16)
    cmov = sb("cmov", [128, 2, 2, 2, H], BF16)
    ddg = sb("ddg", [128, 2, NHT, 128], BF16)
    gamb = sb("gamb", [128, 2, H])
    hb2 = sb("hb2", [128, H])
    idf = sb("idf", [128, 128])
    idb = sb("idb", [128, 128], BF16)
    f1bias = sb("f1bias", [128, NFT])
    dvec = sb("dvec", [128, 2, NHT])
    mu1s = sb("mu1s", [128, NBLK, NSUB])
    r1s = sb("r1s", [128, NBLK, NSUB])
    stat = sb("stat", [128, 16])
    stat2 = sb("stat2", [128, 24])
    stat3 = sb("stat3", [128, 16])
    nst = sb("nst", [128, 8])
    cst4 = sb("cst4", [128, 12])
    svec = sb("svec", [128, 2, 2, 24])
    junk = sb("junk", [128, H])
    qrow = xbuf[:].rearrange("p s h -> p (s h)")
    qbf = ybuf[:, 0:8, :].rearrange("p a b -> p (a b)")
    scbc = rbase[:].rearrange("p a b c t -> p (a b c t)")

    ps = es.enter_context(nc.psum_tensor("ps", [128, 8, 512], F32))

    pl = Plan()
    s_cst = pl.sem("s_cst")
    s_x = pl.sem("s_x")
    s_xnt = pl.sem("s_xnt")
    s_sbb = pl.sem("s_sbb")
    s_w1 = pl.sem("s_w1")
    s_w2 = pl.sem("s_w2")
    s_q = pl.sem("s_q")
    s_scw = pl.sem("s_scw")
    s_out = pl.sem("s_out")
    s_pe = pl.sem("s_pe")
    s_dv = pl.sem("s_dv")
    s_ac = pl.sem("s_ac")
    s_pl = pl.sem("s_pl")

    free = {
        "bank": [(s_pe, 0)] * 8,
        "xbuf": (s_pe, 0), "xnT": (s_pe, 0), "hT": (s_pe, 0),
        "sbb": (s_pe, 0), "ssf": (s_pe, 0),
        "w1": [(s_pe, 0), (s_pe, 0)], "w2": [(s_pe, 0), (s_pe, 0)],
        "obuf": [(s_pe, 0), (s_pe, 0)],
    }

    def dve(fn, waits=()):
        pl.step("vector", fn, waits=waits, incs=[(s_dv, 1)])

    def spacer():
        dve(lambda e: e.tensor_tensor(out=cst4[:, 8:12], in0=cst4[:, 0:4],
                                      in1=cst4[:, 4:8], op=ALU.mult))

    def dves(fn, waits=()):
        dve(fn, waits=waits)
        spacer()

    def act(fn, waits=()):
        pl.step("scalar", fn, waits=waits, incs=[(s_ac, 1)])

    def mm_group(emits, waits=()):
        w = list(waits)
        for fn in emits[:-1]:
            pl.step("tensor", fn, waits=w)
            w = []
        pl.step("tensor", emits[-1], waits=w, incs=[(s_pe, 1)])
        return (s_pe, s_pe.count)

    def newton_rsqrt(v_ap, r_ap):
        v2 = nst[:, 0:4]
        t1 = nst[:, 4:8]
        dves((lambda v_ap: lambda e: e.scalar_tensor_tensor(
            out=v2, in0=v_ap, scalar=1.0, in1=cst4[:, 0:4],
            op0=ALU.mult, op1=ALU.add))(v_ap))
        dves((lambda r_ap: lambda e: e.tensor_scalar(
            out=r_ap.bitcast(I32), in0=v2.bitcast(I32),
            scalar1=1, scalar2=None, op0=ALU.arith_shift_right))(r_ap))
        dves((lambda r_ap: lambda e: e.tensor_scalar(
            out=r_ap.bitcast(I32), in0=r_ap.bitcast(I32),
            scalar1=-1, scalar2=RSQRT_MAGIC, op0=ALU.mult, op1=ALU.add))(r_ap))
        for _ in range(3):
            dves((lambda r_ap: lambda e: e.tensor_tensor(out=t1, in0=r_ap, in1=r_ap, op=ALU.mult))(r_ap))
            dves(lambda e: e.tensor_tensor(out=t1, in0=t1, in1=v2, op=ALU.mult))
            dves(lambda e: e.scalar_tensor_tensor(out=t1, in0=t1, scalar=-0.5,
                                                  in1=cst4[:, 4:8], op0=ALU.mult, op1=ALU.add))
            dves((lambda r_ap: lambda e: e.tensor_tensor(out=r_ap, in0=r_ap, in1=t1, op=ALU.mult))(r_ap))

    LOG2E = float(np.log2(np.e))
    LN2_HI = 0.6933594
    LN2_LO = -0.00021219444005469057

    def soft_exp(dst, srcp, waits=()):
        t = nst[:, 0:1]
        n = nst[:, 1:2]
        r = nst[:, 2:3]
        p = nst[:, 3:4]
        dves((lambda: lambda e: e.tensor_scalar(out=t, in0=srcp, scalar1=LOG2E,
                                                scalar2=None, op0=ALU.mult))(), waits=waits)
        dves(lambda e: e.tensor_copy(out=n.bitcast(I32), in_=t))
        dves(lambda e: e.tensor_copy(out=n, in_=n.bitcast(I32)))
        dves((lambda: lambda e: e.scalar_tensor_tensor(out=r, in0=n, scalar=-LN2_HI,
                                                       in1=srcp, op0=ALU.mult, op1=ALU.add))())
        dves(lambda e: e.scalar_tensor_tensor(out=r, in0=n, scalar=-LN2_LO,
                                              in1=r, op0=ALU.mult, op1=ALU.add))
        dves(lambda e: e.tensor_scalar(out=p, in0=r, scalar1=1.0 / 6, scalar2=1.0,
                                       op0=ALU.mult, op1=ALU.add))
        for dcf in (1.0 / 5, 1.0 / 4, 1.0 / 3, 1.0 / 2, 1.0):
            dves((lambda dcf: lambda e: e.tensor_tensor(out=p, in0=p, in1=r, op=ALU.mult))(dcf))
            dves((lambda dcf: lambda e: e.tensor_scalar(out=p, in0=p, scalar1=dcf,
                                                        scalar2=1.0, op0=ALU.mult, op1=ALU.add))(dcf))
        dves(lambda e: e.tensor_scalar(out=n, in0=n, scalar1=-126.0, scalar2=None,
                                       op0=ALU.max))
        dves(lambda e: e.tensor_scalar(out=n, in0=n, scalar1=127.0, scalar2=None,
                                       op0=ALU.min))
        dves(lambda e: e.tensor_copy(out=t.bitcast(I32), in_=n))
        dves(lambda e: e.tensor_scalar(out=t.bitcast(I32), in0=t.bitcast(I32),
                                       scalar1=127, scalar2=None, op0=ALU.add))
        dves(lambda e: e.tensor_scalar(out=t.bitcast(I32), in0=t.bitcast(I32),
                                       scalar1=23, scalar2=None, op0=ALU.logical_shift_left))
        dves((lambda: lambda e: e.tensor_tensor(out=dst, in0=p, in1=t, op=ALU.mult))())

    def stats_tail(mu_ap, r_ap, waits):
        dves((lambda mu_ap: lambda e: e.tensor_scalar(
            out=mu_ap, in0=stat[:, 0:4], scalar1=INV_H, scalar2=None,
            op0=ALU.mult))(mu_ap), waits=waits)
        dves((lambda mu_ap: lambda e: e.tensor_tensor(
            out=stat[:, 12:16], in0=mu_ap, in1=mu_ap, op=ALU.mult))(mu_ap))
        dves(lambda e: e.scalar_tensor_tensor(out=stat[:, 8:12], in0=stat[:, 4:8],
                                              scalar=INV_H, in1=stat[:, 12:16],
                                              op0=ALU.mult, op1=ALU.subtract))
        newton_rsqrt(stat[:, 8:12], r_ap)

    def prep():
        loads = [
            (idf[:], idf_in), (idb[:], idb_in),
            (gamb[:, 0, :], ln_g[None, :].partition_broadcast(128)),
            (gamb[:, 1, :], ln_b[None, :].partition_broadcast(128)),
            (hb2[:], fc2_b[None, :].partition_broadcast(128)),
        ]
        for dst, src in loads:
            pl.step("sync", (lambda d, s: (lambda e: e.dma_start(out=d, in_=s)))(dst, src),
                    incs=[(s_cst, 16)])
        for i in range(NFT):
            pl.step("sync", (lambda i: (lambda e: e.dma_start(
                out=f1bias[:, i:i + 1], in_=fc1_b[i * 128:(i + 1) * 128].unsqueeze(1))))(i),
                incs=[(s_cst, 16)])
        for di, d in enumerate(("fwd", "bwd")):
            for i in range(NHT):
                pl.step("sync", (lambda d, di, i: (lambda e: e.dma_start(
                    out=dvec[:, di, i:i + 1],
                    in_=prm[f"{d}_D"][i * 128:(i + 1) * 128].unsqueeze(1))))(d, di, i),
                    incs=[(s_cst, 16)])
            for pb in range(2):
                for sl, nm in ((0, "log_real"), (1, "imag"), (2, "log_Delta")):
                    pl.step("sync", (lambda d, di, pb, sl, nm: (lambda e: e.dma_start(
                        out=svec[:, di, pb, sl:sl + 1],
                        in_=prm[f"{d}_{nm}"][pb * 128:(pb + 1) * 128].unsqueeze(1))))(d, di, pb, sl, nm),
                        incs=[(s_cst, 16)])
        cst_all = s_cst.count

        dve(lambda e: e.memset(cst4[:, 0:4], EPS))
        dve(lambda e: e.memset(cst4[:, 4:8], 1.5))
        dve(lambda e: e.memset(cst4[:, 8:12], 0.0))

        first = [True]
        for di in range(2):
            for pb in range(2):
                S = (lambda sl, di=di, pb=pb: svec[:, di, pb, sl:sl + 1])
                w0 = [(s_cst, cst_all)] if first[0] else []
                first[0] = False
                soft_exp(S(3), S(0), waits=w0)
                soft_exp(S(4), S(2))
        for di in range(2):
            for pb in range(2):
                S = (lambda sl, di=di, pb=pb: svec[:, di, pb, sl:sl + 1])
                w0 = []
                dves((lambda S: lambda e: e.scalar_tensor_tensor(
                    out=S(5), in0=S(3), scalar=-1.0, in1=S(4),
                    op0=ALU.mult, op1=ALU.mult))(S), waits=w0)
                dves((lambda S: lambda e: e.tensor_tensor(out=S(6), in0=S(1), in1=S(4), op=ALU.mult))(S))
                dves((lambda S: lambda e: e.tensor_scalar(out=S(22), in0=S(6), scalar1=INV_2PI,
                                                          scalar2=None, op0=ALU.mult))(S))
                dves((lambda S: lambda e: e.tensor_copy(out=S(8).bitcast(I32), in_=S(22)))(S))
                dves((lambda S: lambda e: e.tensor_copy(out=S(8), in_=S(8).bitcast(I32)))(S))
                dves((lambda S: lambda e: e.scalar_tensor_tensor(
                    out=S(23), in0=S(8), scalar=-TWO_PI_HI, in1=S(6),
                    op0=ALU.mult, op1=ALU.add))(S))
                dves((lambda S: lambda e: e.scalar_tensor_tensor(
                    out=S(9), in0=S(8), scalar=-TWO_PI_LO, in1=S(23),
                    op0=ALU.mult, op1=ALU.add))(S))
                dves((lambda S: lambda e: e.tensor_scalar(out=S(22), in0=S(9), scalar1=0.25,
                                                          scalar2=None, op0=ALU.mult))(S))
                dves((lambda S: lambda e: e.tensor_tensor(out=S(23), in0=S(22), in1=S(22), op=ALU.mult))(S))
                for k, c in enumerate([1.0 / 362880, -1.0 / 5040, 1.0 / 120, -1.0 / 6]):
                    if k == 0:
                        dves((lambda S, c=c: lambda e: e.tensor_scalar(
                            out=S(10), in0=S(23), scalar1=c, scalar2=None, op0=ALU.mult))(S))
                    else:
                        dves((lambda S, c=c: lambda e: e.tensor_scalar(
                            out=S(10), in0=S(10), scalar1=c, scalar2=None, op0=ALU.add))(S))
                        dves((lambda S: lambda e: e.tensor_tensor(
                            out=S(10), in0=S(10), in1=S(23), op=ALU.mult))(S))
                dves((lambda S: lambda e: e.tensor_scalar(out=S(10), in0=S(10), scalar1=1.0,
                                                          scalar2=None, op0=ALU.add))(S))
                dves((lambda S: lambda e: e.tensor_tensor(out=S(10), in0=S(10), in1=S(22), op=ALU.mult))(S))
                for k, c in enumerate([1.0 / 40320, -1.0 / 720, 1.0 / 24, -0.5]):
                    if k == 0:
                        dves((lambda S, c=c: lambda e: e.tensor_scalar(
                            out=S(11), in0=S(23), scalar1=c, scalar2=None, op0=ALU.mult))(S))
                    else:
                        dves((lambda S, c=c: lambda e: e.tensor_scalar(
                            out=S(11), in0=S(11), scalar1=c, scalar2=None, op0=ALU.add))(S))
                        dves((lambda S: lambda e: e.tensor_tensor(
                            out=S(11), in0=S(11), in1=S(23), op=ALU.mult))(S))
                dves((lambda S: lambda e: e.tensor_scalar(out=S(11), in0=S(11), scalar1=1.0,
                                                          scalar2=None, op0=ALU.add))(S))
                for _ in range(2):
                    dves((lambda S: lambda e: e.tensor_tensor(out=S(22), in0=S(11), in1=S(11), op=ALU.mult))(S))
                    dves((lambda S: lambda e: e.tensor_tensor(out=S(23), in0=S(10), in1=S(10), op=ALU.mult))(S))
                    dves((lambda S: lambda e: e.tensor_tensor(out=S(10), in0=S(10), in1=S(11), op=ALU.mult))(S))
                    dves((lambda S: lambda e: e.tensor_scalar(out=S(10), in0=S(10), scalar1=2.0,
                                                              scalar2=None, op0=ALU.mult))(S))
                    dves((lambda S: lambda e: e.tensor_tensor(out=S(11), in0=S(22), in1=S(23), op=ALU.subtract))(S))
        for di in range(2):
            for pb in range(2):
                S = (lambda sl, di=di, pb=pb: svec[:, di, pb, sl:sl + 1])
                soft_exp(S(7), S(5))
        for di in range(2):
            for pb in range(2):
                S = (lambda sl, di=di, pb=pb: svec[:, di, pb, sl:sl + 1])
                dves((lambda S: lambda e: e.tensor_tensor(out=S(12), in0=S(7), in1=S(11), op=ALU.mult))(S))
                dves((lambda S: lambda e: e.tensor_tensor(out=S(13), in0=S(7), in1=S(10), op=ALU.mult))(S))
                dves((lambda S: lambda e: e.tensor_copy(out=S(16), in_=S(11)))(S))
                dves((lambda S: lambda e: e.tensor_scalar(out=S(17), in0=S(10), scalar1=-1.0,
                                                          scalar2=None, op0=ALU.mult))(S))
                dves((lambda S: lambda e: e.tensor_tensor(out=S(22), in0=S(1), in1=S(1), op=ALU.mult))(S))
                dves((lambda S: lambda e: e.tensor_tensor(out=S(23), in0=S(3), in1=S(3), op=ALU.mult))(S))
                dves((lambda S: lambda e: e.tensor_tensor(out=S(22), in0=S(22), in1=S(23), op=ALU.add))(S))
                dves((lambda S: lambda e: e.tensor_scalar(out=S(23), in0=S(22), scalar1=1e-12,
                                                          scalar2=None, op0=ALU.is_lt))(S))
                dves((lambda S: lambda e: e.tensor_scalar(out=S(21), in0=S(23), scalar1=-1.0,
                                                          scalar2=1.0, op0=ALU.mult, op1=ALU.add))(S))
                dves((lambda S: lambda e: e.tensor_scalar(out=S(18), in0=S(3), scalar1=-1.0,
                                                          scalar2=None, op0=ALU.mult))(S))
                dves((lambda S: lambda e: e.tensor_tensor(out=S(18), in0=S(18), in1=S(21), op=ALU.mult))(S))
                dves((lambda S: lambda e: e.tensor_scalar(out=S(20), in0=S(23), scalar1=1e-6,
                                                          scalar2=None, op0=ALU.mult))(S))
                dves((lambda S: lambda e: e.tensor_tensor(out=S(18), in0=S(18), in1=S(20), op=ALU.add))(S))
                dves((lambda S: lambda e: e.tensor_tensor(out=S(19), in0=S(21), in1=S(1), op=ALU.mult))(S))
                dves((lambda S: lambda e: e.tensor_tensor(out=S(22), in0=S(18), in1=S(18), op=ALU.mult))(S))
                dves((lambda S: lambda e: e.tensor_tensor(out=S(23), in0=S(19), in1=S(19), op=ALU.mult))(S))
                dves((lambda S: lambda e: e.tensor_tensor(out=S(22), in0=S(22), in1=S(23), op=ALU.add))(S))
                dves((lambda S: lambda e: e.reciprocal(out=S(22), in_=S(22)))(S))
                dves((lambda S: lambda e: e.tensor_scalar(out=S(23), in0=S(12), scalar1=-1.0,
                                                          scalar2=None, op0=ALU.add))(S))
                dves((lambda S: lambda e: e.tensor_tensor(out=S(14), in0=S(23), in1=S(18), op=ALU.mult))(S))
                dves((lambda S: lambda e: e.tensor_tensor(out=S(20), in0=S(13), in1=S(19), op=ALU.mult))(S))
                dves((lambda S: lambda e: e.tensor_tensor(out=S(14), in0=S(14), in1=S(20), op=ALU.add))(S))
                dves((lambda S: lambda e: e.tensor_tensor(out=S(14), in0=S(14), in1=S(22), op=ALU.mult))(S))
                dves((lambda S: lambda e: e.tensor_tensor(out=S(15), in0=S(13), in1=S(18), op=ALU.mult))(S))
                dves((lambda S: lambda e: e.tensor_tensor(out=S(20), in0=S(23), in1=S(19), op=ALU.mult))(S))
                dves((lambda S: lambda e: e.tensor_tensor(out=S(15), in0=S(15), in1=S(20), op=ALU.subtract))(S))
                dves((lambda S: lambda e: e.tensor_tensor(out=S(15), in0=S(15), in1=S(22), op=ALU.mult))(S))

        for di in range(2):
            for i in range(NHT):
                dve((lambda di, i: lambda e: e.tensor_scalar(
                    out=ddg[:, di, i, :], in0=idb[:], scalar1=dvec[:, di, i:i + 1],
                    scalar2=None, op0=ALU.mult))(di, i))
        dve(lambda e: e.tensor_scalar(out=hb2[:], in0=hb2[:], scalar1=0.5, scalar2=None, op0=ALU.mult))
        spacer()

        for di, d in enumerate(("fwd", "bwd")):
            for pb in range(2):
                S = (lambda sl, di=di, pb=pb: svec[:, di, pb, sl:sl + 1])
                pl.step("sync", (lambda d, pb: lambda e: e.dma_start(
                    out=qrow[:, 0:H], in_=prm[f"{d}_B_real"][pb * 128:(pb + 1) * 128, :]))(d, pb),
                    waits=[(s_dv, s_dv.count), (s_pe, s_pe.count)], incs=[(s_cst, 16)])
                pl.step("sync", (lambda d, pb: lambda e: e.dma_start(
                    out=qrow[:, H:2 * H], in_=prm[f"{d}_B_imag"][pb * 128:(pb + 1) * 128, :]))(d, pb),
                    incs=[(s_cst, 16)])
                c_now = s_cst.count
                pe_prev = s_pe.count
                dve((lambda S: lambda e: e.tensor_scalar(
                    out=scbc[:, 0:H], in0=qrow[:, 0:H], scalar1=S(14), scalar2=None,
                    op0=ALU.mult))(S), waits=[(s_cst, c_now), (s_pe, pe_prev)])
                dve((lambda S: lambda e: e.tensor_scalar(
                    out=scbc[:, H:2 * H], in0=qrow[:, H:2 * H], scalar1=S(15), scalar2=None,
                    op0=ALU.mult))(S))
                dve(lambda e: e.tensor_tensor(out=scbc[:, 0:H], in0=scbc[:, 0:H],
                                              in1=scbc[:, H:2 * H], op=ALU.subtract))
                dve((lambda S: lambda e: e.tensor_scalar(
                    out=scbc[:, H:2 * H], in0=qrow[:, H:2 * H], scalar1=S(14), scalar2=None,
                    op0=ALU.mult))(S))
                dve((lambda S: lambda e: e.tensor_scalar(
                    out=scbc[:, 2 * H:3 * H], in0=qrow[:, 0:H], scalar1=S(15), scalar2=None,
                    op0=ALU.mult))(S))
                dve(lambda e: e.tensor_tensor(out=scbc[:, H:2 * H], in0=scbc[:, H:2 * H],
                                              in1=scbc[:, 2 * H:3 * H], op=ALU.add))
                dvn = s_dv.count
                for comp in range(2):
                    for kt in range(NHT):
                        bank = kt % 2
                        w = [(s_dv, dvn)] if (comp == 0 and kt == 0) else []
                        w.append(free["bank"][bank])
                        pe_t = mm_group([(lambda comp, kt, bank: lambda e: e.transpose(
                            ps[:, bank, 0:128],
                            scbc[:, comp * H + kt * 128:comp * H + (kt + 1) * 128],
                            idf[:]))(comp, kt, bank)], waits=w)
                        act((lambda di, pb, comp, kt, bank: lambda e: e.activation(
                            out=btst[:, di, kt, comp, pb * 128:(pb + 1) * 128],
                            in_=ps[:, bank, 0:128], func=AF.Copy))(di, pb, comp, kt, bank),
                            waits=[pe_t])
                        free["bank"][bank] = (s_ac, s_ac.count)

        for di, d in enumerate(("fwd", "bwd")):
            for comp, nm in ((0, "C_real"), (1, "C_imag")):
                for ht in range(NHT):
                    pl.step("sync", (lambda d, nm, ht: lambda e: e.dma_start(
                        out=qrow[:, 0:P], in_=prm[f"{d}_{nm}"][ht * 128:(ht + 1) * 128, :]))(d, nm, ht),
                        waits=[(s_pe, s_pe.count)], incs=[(s_cst, 16)])
                    c_now = s_cst.count
                    for pb in range(2):
                        bank = 2 + pb
                        w = [(s_cst, c_now), free["bank"][bank]]
                        pe_t = mm_group([(lambda pb, bank: lambda e: e.transpose(
                            ps[:, bank, 0:128], qrow[:, pb * 128:(pb + 1) * 128], idf[:]))(pb, bank)],
                            waits=w)
                        sgn = -1.0 if comp == 1 else 1.0
                        act((lambda di, pb, comp, ht, bank, sgn: lambda e: e.activation(
                            out=cmov[:, di, pb, comp, ht * 128:(ht + 1) * 128],
                            in_=ps[:, bank, 0:128], func=AF.Copy, scale=sgn))(di, pb, comp, ht, bank, sgn),
                            waits=[pe_t])
                        free["bank"][bank] = (s_ac, s_ac.count)

        for kt in list(range(NHT)) + [0]:
            pl.step("gpsimd", (lambda kt: lambda e: e.dma_start(
                out=qrow[:], in_=fc1_q[kt * 128:(kt + 1) * 128, :]))(kt),
                waits=[(s_dv, s_dv.count), (s_scw, s_scw.count)], incs=[(s_q, 16)])
            pl.step("sync", (lambda kt: lambda e: e.dma_start(
                out=scbc[:, 0:FF],
                in_=fc1_sc[kt, :][None, :].partition_broadcast(128)))(kt),
                waits=[(s_dv, s_dv.count), (s_pe, s_pe.count)], incs=[(s_cst, 16)])
            dve(lambda e: e.scalar_tensor_tensor(out=qbf[:], in0=qrow[:], scalar=-128.0,
                                                 in1=scbc[:, 0:FF], op0=ALU.add, op1=ALU.mult),
                waits=[(s_q, s_q.count), (s_cst, s_cst.count)])
            spacer()
            spacer()
            for ft in range(NFT):
                pl.step("gpsimd", (lambda kt, ft: lambda e: e.dma_start(
                    out=w1s[ft, :, kt, :], in_=qbf[:, ft * 128:(ft + 1) * 128]))(kt, ft),
                    waits=[(s_dv, s_dv.count)] if ft == 0 else [], incs=[(s_scw, 16)])
        for kt in range(NFT):
            pl.step("gpsimd", (lambda kt: lambda e: e.dma_start(
                out=qrow[:, 0:H], in_=fc2_q[kt * 128:(kt + 1) * 128, :]))(kt),
                waits=[(s_dv, s_dv.count), (s_scw, s_scw.count)], incs=[(s_q, 16)])
            pl.step("sync", (lambda kt: lambda e: e.dma_start(
                out=scbc[:, 0:H], in_=fc2_sc[kt, :][None, :].partition_broadcast(128)))(kt),
                waits=[(s_dv, s_dv.count)], incs=[(s_cst, 16)])
            dve(lambda e: e.scalar_tensor_tensor(out=qbf[:, 0:H], in0=qrow[:, 0:H], scalar=-128.0,
                                                 in1=scbc[:, 0:H], op0=ALU.add, op1=ALU.mult),
                waits=[(s_q, s_q.count), (s_cst, s_cst.count)])
            spacer()
            spacer()
            pl.step("gpsimd", (lambda kt: lambda e: e.dma_start(
                out=w2s[kt], in_=qbf[:, 0:H]))(kt),
                waits=[(s_dv, s_dv.count)], incs=[(s_scw, 16)])

        for di in range(2):
            for pb in range(2):
                S = (lambda sl, di=di, pb=pb: svec[:, di, pb, sl:sl + 1])
                rr = rbase[:, di, pb, 0, :]
                ri = rbase[:, di, pb, 1, :]
                w0 = [(s_cst, s_cst.count), (s_q, s_q.count), (s_scw, s_scw.count)] \
                    if (di == 0 and pb == 0) else []
                dves((lambda rr=rr: lambda e: e.memset(rr[:, 0:1], 1.0))(), waits=w0)
                dves((lambda ri=ri: lambda e: e.memset(ri[:, 0:1], 0.0))())
                dves((lambda S: lambda e: e.tensor_copy(out=S(20), in_=S(16)))(S))
                dves((lambda S: lambda e: e.tensor_copy(out=S(21), in_=S(17)))(S))
                m = 1
                while m < TB:
                    dves((lambda rr, S, m: lambda e: e.tensor_scalar(
                        out=tmpa[:, 0:m], in0=rr[:, 0:m], scalar1=S(20), scalar2=None,
                        op0=ALU.mult))(rr, S, m))
                    dves((lambda ri, S, m: lambda e: e.tensor_scalar(
                        out=tmpb[:, 0:m], in0=ri[:, 0:m], scalar1=S(21), scalar2=None,
                        op0=ALU.mult))(ri, S, m))
                    dves((lambda rr, m: lambda e: e.tensor_tensor(
                        out=rr[:, m:2 * m], in0=tmpa[:, 0:m], in1=tmpb[:, 0:m],
                        op=ALU.subtract))(rr, m))
                    dves((lambda rr, S, m: lambda e: e.tensor_scalar(
                        out=tmpa[:, 0:m], in0=rr[:, 0:m], scalar1=S(21), scalar2=None,
                        op0=ALU.mult))(rr, S, m))
                    dves((lambda ri, S, m: lambda e: e.tensor_scalar(
                        out=tmpb[:, 0:m], in0=ri[:, 0:m], scalar1=S(20), scalar2=None,
                        op0=ALU.mult))(ri, S, m))
                    dves((lambda ri, m: lambda e: e.tensor_tensor(
                        out=ri[:, m:2 * m], in0=tmpa[:, 0:m], in1=tmpb[:, 0:m],
                        op=ALU.add))(ri, m))
                    if 2 * m < TB:
                        dves((lambda S: lambda e: e.tensor_tensor(out=S(22), in0=S(20), in1=S(20), op=ALU.mult))(S))
                        dves((lambda S: lambda e: e.tensor_tensor(out=S(23), in0=S(21), in1=S(21), op=ALU.mult))(S))
                        dves((lambda S: lambda e: e.tensor_tensor(out=S(21), in0=S(21), in1=S(20), op=ALU.mult))(S))
                        dves((lambda S: lambda e: e.tensor_tensor(out=S(20), in0=S(22), in1=S(23), op=ALU.subtract))(S))
                        dves((lambda S: lambda e: e.tensor_scalar(out=S(21), in0=S(21), scalar1=2.0,
                                                                  scalar2=None, op0=ALU.mult))(S))
                    m *= 2
                dves((lambda rr, S: lambda e: e.tensor_copy(out=S(22), in_=rr[:, TB - 1:TB]))(rr, S))
                dves((lambda ri, S: lambda e: e.tensor_copy(out=S(23), in_=ri[:, TB - 1:TB]))(ri, S))
                dves((lambda S: lambda e: e.tensor_tensor(out=S(18), in0=S(22), in1=S(16), op=ALU.mult))(S))
                dves((lambda S: lambda e: e.tensor_tensor(out=S(20), in0=S(23), in1=S(17), op=ALU.mult))(S))
                dves((lambda S: lambda e: e.tensor_tensor(out=S(18), in0=S(18), in1=S(20), op=ALU.subtract))(S))
                dves((lambda S: lambda e: e.tensor_tensor(out=S(19), in0=S(22), in1=S(17), op=ALU.mult))(S))
                dves((lambda S: lambda e: e.tensor_tensor(out=S(20), in0=S(23), in1=S(16), op=ALU.mult))(S))
                dves((lambda S: lambda e: e.tensor_tensor(out=S(19), in0=S(19), in1=S(20), op=ALU.add))(S))

    def ln1_stats_act(j, waits):
        for s in range(NSUB):
            act((lambda s: lambda e: e.activation(
                out=junk[:], in_=xbuf[:, s, :], func=AF.Copy,
                accum_out=stat[:, s:s + 1]))(s), waits=(waits if s == 0 else []))
            act((lambda s: lambda e: e.activation(
                out=junk[:], in_=xbuf[:, s, :], func=AF.Square,
                accum_out=stat[:, 4 + s:5 + s]))(s))
        a_now = s_ac.count
        stats_tail(mu1s[:, j, :], r1s[:, j, :], waits=[(s_ac, a_now)])
        return (s_dv, s_dv.count)

    def ln2_stats_dve(waits):
        for s in range(NSUB):
            dve((lambda s: lambda e: e.tensor_reduce(
                out=stat[:, s:s + 1], in_=xbuf[:, s, :], axis=AX.X, op=ALU.add))(s),
                waits=(waits if s == 0 else []))
            dve((lambda s: lambda e: e.scalar_tensor_tensor(
                out=junk[:], in0=xbuf[:, s, :], scalar=0.0, in1=xbuf[:, s, :],
                op0=ALU.add, op1=ALU.mult, accum_out=stat[:, 4 + s:5 + s]))(s))
        stats_tail(stat2[:, 0:4], stat2[:, 4:8], waits=[])

    def pow_init(di):
        for pb in range(2):
            S = (lambda sl, pb=pb: svec[:, di, pb, sl:sl + 1])
            dves((lambda S: lambda e: e.memset(S(20), 1.0))(S))
            dves((lambda S: lambda e: e.memset(S(21), 0.0))(S))

    def pow_update(di):
        for pb in range(2):
            S = (lambda sl, pb=pb: svec[:, di, pb, sl:sl + 1])
            dves((lambda S: lambda e: e.tensor_tensor(out=S(22), in0=S(20), in1=S(18), op=ALU.mult))(S))
            dves((lambda S: lambda e: e.tensor_tensor(out=S(23), in0=S(21), in1=S(19), op=ALU.mult))(S))
            dves((lambda S: lambda e: e.tensor_tensor(out=S(21), in0=S(21), in1=S(18), op=ALU.mult))(S))
            dves((lambda S: lambda e: e.tensor_tensor(out=S(22), in0=S(22), in1=S(23), op=ALU.subtract))(S))
            dves((lambda S: lambda e: e.tensor_tensor(out=S(23), in0=S(20), in1=S(19), op=ALU.mult))(S))
            dves((lambda S: lambda e: e.tensor_tensor(out=S(21), in0=S(21), in1=S(23), op=ALU.add))(S))
            dves((lambda S: lambda e: e.tensor_copy(out=S(20), in_=S(22)))(S))

    def build_F(di):
        for pb in range(2):
            S = (lambda sl, pb=pb: svec[:, di, pb, sl:sl + 1])
            rr = rbase[:, di, pb, 0, :]
            ri = rbase[:, di, pb, 1, :]
            fr = fpl[:, pb * 2 + 0, :]
            fi = fpl[:, pb * 2 + 1, :]
            dve((lambda S, rr: lambda e: e.tensor_scalar(
                out=tmpa[:], in0=rr, scalar1=S(20), scalar2=None, op0=ALU.mult))(S, rr))
            dve((lambda S, ri: lambda e: e.tensor_scalar(
                out=tmpb[:], in0=ri, scalar1=S(21), scalar2=None, op0=ALU.mult))(S, ri))
            dve((lambda fr: lambda e: e.tensor_tensor(
                out=fr, in0=tmpa[:], in1=tmpb[:], op=ALU.subtract))(fr))
            dve((lambda S, rr: lambda e: e.tensor_scalar(
                out=tmpa[:], in0=rr, scalar1=S(21), scalar2=None, op0=ALU.mult))(S, rr))
            dve((lambda S, ri: lambda e: e.tensor_scalar(
                out=tmpb[:], in0=ri, scalar1=S(20), scalar2=None, op0=ALU.mult))(S, ri))
            dve((lambda fi: lambda e: e.tensor_tensor(
                out=fi, in0=tmpa[:], in1=tmpb[:], op=ALU.add))(fi))

    def rotate_scan_unrotate(di, first_block, store_rev, pe_bu, s_free):
        for pb in range(2):
            fr = fpl[:, pb * 2 + 0, :]
            fi = fpl[:, pb * 2 + 1, :]
            br = ps[:, pb * 2 + 0, :]
            bi = ps[:, pb * 2 + 1, :]
            w0 = [pe_bu] if pb == 0 else []
            dve((lambda fr, br: lambda e: e.tensor_tensor(out=tmpa[:], in0=fr, in1=br, op=ALU.mult))(fr, br), waits=w0)
            dve((lambda fi, bi: lambda e: e.tensor_tensor(out=tmpb[:], in0=fi, in1=bi, op=ALU.mult))(fi, bi))
            dve((lambda pb: lambda e: e.tensor_tensor(
                out=bpr[:, pb * 2 + 0, :], in0=tmpa[:], in1=tmpb[:], op=ALU.subtract))(pb))
            dve((lambda fr, bi: lambda e: e.tensor_tensor(out=tmpa[:], in0=fr, in1=bi, op=ALU.mult))(fr, bi))
            dve((lambda fi, br: lambda e: e.tensor_tensor(out=tmpb[:], in0=fi, in1=br, op=ALU.mult))(fi, br))
            dve((lambda pb: lambda e: e.tensor_tensor(
                out=bpr[:, pb * 2 + 1, :], in0=tmpa[:], in1=tmpb[:], op=ALU.add))(pb))
        dv_rot = (s_dv, s_dv.count)
        for pc in range(4):
            pb = pc // 2
            rho = svec[:, di, pb, 7:8].broadcast_to([128, TB])
            if first_block:
                dve((lambda pc, rho: lambda e: e.tensor_tensor_scan(
                    out=wpl[:, pc, :], data0=rho, data1=bpr[:, pc, :], initial=0.0,
                    op0=ALU.mult, op1=ALU.add))(pc, rho))
            else:
                dve((lambda pc, rho: lambda e: e.tensor_tensor_scan(
                    out=wpl[:, pc, :], data0=rho, data1=bpr[:, pc, :],
                    initial=wpl[:, pc, TB - 1:TB], op0=ALU.mult, op1=ALU.add))(pc, rho))
        for pb in range(2):
            fr = fpl[:, pb * 2 + 0, :]
            fi = fpl[:, pb * 2 + 1, :]
            wr = wpl[:, pb * 2 + 0, :]
            wi = wpl[:, pb * 2 + 1, :]
            for comp in range(2):
                e1, e2, op = (wr, wi, ALU.add) if comp == 0 else (wi, wr, ALU.subtract)
                dst = ssf[:, pb * 2 + comp, ::-1] if store_rev else ssf[:, pb * 2 + comp, :]
                w0 = [s_free] if (pb == 0 and comp == 0) else []
                dve((lambda fr, e1: lambda e: e.tensor_tensor(out=tmpa[:], in0=fr, in1=e1, op=ALU.mult))(fr, e1), waits=w0)
                dve((lambda fi, e2: lambda e: e.tensor_tensor(out=tmpb[:], in0=fi, in1=e2, op=ALU.mult))(fi, e2))
                dve((lambda dst, op: lambda e: e.tensor_tensor(out=dst, in0=tmpa[:], in1=tmpb[:], op=op))(dst, op))
        return dv_rot, (s_dv, s_dv.count)

    def pass_a(b):
        pow_init(1)
        for J in range(NBLK):
            j = NBLK - 1 - J
            pl.step("sync", (lambda b, j: lambda e: e.dma_start(
                out=xbuf[:], in_=x_in[b, j * TB:(j + 1) * TB, :].rearrange(
                    "(s p) h -> p s h", p=128)))(b, j),
                waits=[free["xbuf"]], incs=[(s_x, 16)])
            x_now = s_x.count
            ln1_stats_act(j, waits=[(s_x, x_now)])
            for s in range(NSUB):
                hs = s % 2
                w = []
                if s >= 2:
                    w.append((s_pe, s_pe.count))
                dve((lambda s, hs, j: lambda e: e.tensor_scalar(
                    out=hbuf[:, hs, :], in0=xbuf[:, s, :],
                    scalar1=mu1s[:, j, s:s + 1], scalar2=r1s[:, j, s:s + 1],
                    op0=ALU.subtract, op1=ALU.mult))(s, hs, j), waits=w)
                dv_xn = s_dv.count
                for g4 in range(2):
                    bank = 4 + g4
                    emits = []
                    for s4 in range(4):
                        ht = g4 * 4 + s4
                        emits.append((lambda hs, ht, bank, s4: lambda e: e.transpose(
                            ps[:, bank, s4 * 128:(s4 + 1) * 128],
                            hbuf[:, hs, ht * 128:(ht + 1) * 128], idf[:]))(hs, ht, bank, s4))
                    w = [free["bank"][bank]]
                    if g4 == 0:
                        w.append((s_dv, dv_xn))
                    pe_t = mm_group(emits, waits=w)
                    w = [pe_t]
                    if s == 0 and g4 == 0:
                        w += [free["xnT"], free["hT"]]
                    act((lambda g4, s, bank: lambda e: e.activation(
                        out=xnT[:, g4 * 4:(g4 + 1) * 4, s * 128:(s + 1) * 128],
                        in_=ps[:, bank, :].rearrange("p (s4 c) -> p s4 c", c=128),
                        func=AF.Copy))(g4, s, bank), waits=w)
                    act((lambda g4, s, bank: lambda e: e.activation(
                        out=hT[:, g4 * 4:(g4 + 1) * 4, (NSUB - 1 - s) * 128:(NSUB - s) * 128],
                        in_=ps[:, bank, :].rearrange("p (s4 c) -> p s4 c", c=128)[:, :, ::-1],
                        func=AF.Copy))(g4, s, bank))
                    free["bank"][bank] = (s_ac, s_ac.count)
            ac_evac = s_ac.count
            pl.step("gpsimd", (lambda b, j: lambda e: e.dma_start(
                out=xnt_s[b, j], in_=xnT[:]))(b, j),
                waits=[(s_ac, ac_evac)], incs=[(s_scw, 16)])
            scw_xnt = s_scw.count
            free["xnT"] = (s_scw, scw_xnt)
            pe_bu = None
            for pb in range(2):
                for comp in range(2):
                    bank = pb * 2 + comp
                    emits = []
                    for kt in range(NHT):
                        emits.append((lambda pb, comp, kt, bank: lambda e: e.matmul(
                            ps[:, bank, :], btst[:, 1, kt, comp, pb * 128:(pb + 1) * 128],
                            hT[:, kt, :], start=(kt == 0), stop=(kt == NHT - 1)))(pb, comp, kt, bank))
                    w = [free["bank"][bank]]
                    if pb == 0 and comp == 0:
                        w += [(s_ac, ac_evac), (s_scw, scw_xnt)]
                    pe_bu = mm_group(emits, waits=w)
            free["hT"] = pe_bu
            build_F(1)
            dv_rot, dv_unrot = rotate_scan_unrotate(1, first_block=(J == 0),
                                                    store_rev=True, pe_bu=pe_bu,
                                                    s_free=free["ssf"])
            for k in range(4):
                free["bank"][k] = dv_rot
            for pc in range(4):
                pb, comp = pc // 2, pc % 2
                pl.step("gpsimd", (lambda b, pb, comp, j, pc: lambda e: e.dma_start(
                    out=xsb[b, pb, comp, :, j * TB:(j + 1) * TB], in_=ssf[:, pc, :]))(b, pb, comp, j, pc),
                    waits=[dv_unrot] if pc == 0 else [], incs=[(s_scw, 16)])
            free["ssf"] = (s_scw, s_scw.count)
            free["xbuf"] = dv_rot
            pow_update(1)

    def pass_b(b):
        pow_init(0)

        def load_x(j):
            pl.step("sync", (lambda b, j: lambda e: e.dma_start(
                out=xbuf[:], in_=x_in[b, j * TB:(j + 1) * TB, :].rearrange(
                    "(s p) h -> p s h", p=128)))(b, j),
                waits=[free["xbuf"]], incs=[(s_x, 16)])
            return s_x.count

        def attn_pre(j):
            pl.step("sync", (lambda b, j: lambda e: e.dma_start(
                out=xnT[:], in_=xnt_s[b, j]))(b, j),
                waits=[(s_scw, s_scw.count), free["xnT"]], incs=[(s_xnt, 16)])
            xnt_now = s_xnt.count
            pl.step("sync", (lambda b, j: lambda e: e.dma_start(
                out=sbb[:], in_=xsb[b, :, :, :, j * TB:(j + 1) * TB].rearrange(
                    "pb c p t -> p (pb c) t")))(b, j),
                waits=[free["sbb"]], incs=[(s_sbb, 16)])
            sbb_now = s_sbb.count
            pe_bu = None
            for pb in range(2):
                for comp in range(2):
                    bank = pb * 2 + comp
                    emits = []
                    for kt in range(NHT):
                        emits.append((lambda pb, comp, kt, bank: lambda e: e.matmul(
                            ps[:, bank, :], btst[:, 0, kt, comp, pb * 128:(pb + 1) * 128],
                            xnT[:, kt, :], start=(kt == 0), stop=(kt == NHT - 1)))(pb, comp, kt, bank))
                    w = [free["bank"][bank]]
                    if pb == 0 and comp == 0:
                        w.append((s_xnt, xnt_now))
                    pe_bu = mm_group(emits, waits=w)
            build_F(0)
            pow_update(0)
            dv_rot, dv_unrot = rotate_scan_unrotate(0, first_block=(j == 0),
                                                    store_rev=False, pe_bu=pe_bu,
                                                    s_free=free["ssf"])
            for k in range(4):
                free["bank"][k] = dv_rot
            return dv_unrot, sbb_now

        state = {}
        state[0] = attn_pre(0)

        for j in range(NBLK):
            dv_unrot, sbb_now = state.pop(j)
            x_now = load_x(j)

            last_pe = None
            for dpass in range(2):
                di = dpass
                splanes = ssf if dpass == 0 else sbb
                s_ready = dv_unrot if dpass == 0 else (s_sbb, sbb_now)
                for tsub in range(NSUB):
                    for hch in range(2):
                        bank = 2 + (tsub * 2 + hch) % 2
                        emits = []
                        for pb in range(2):
                            for comp in range(2):
                                emits.append((lambda splanes, di, pb, comp, tsub, hch, bank:
                                              lambda e: e.matmul(
                                                  ps[:, bank, :],
                                                  splanes[:, pb * 2 + comp, tsub * 128:(tsub + 1) * 128],
                                                  cmov[:, di, pb, comp, hch * 512:(hch + 1) * 512],
                                                  start=(pb == 0 and comp == 0), stop=False,
                                                  skip_group_check=True))(splanes, di, pb, comp, tsub, hch, bank))
                        for i4 in range(4):
                            ht = hch * 4 + i4
                            emits.append((lambda di, ht, tsub, i4, bank: lambda e: e.matmul(
                                ps[:, bank, i4 * 128:(i4 + 1) * 128],
                                xnT[:, ht, tsub * 128:(tsub + 1) * 128],
                                ddg[:, di, ht, :], start=False, stop=(i4 == 3),
                                skip_group_check=True))(di, ht, tsub, i4, bank))
                        w = [free["bank"][bank]]
                        if tsub == 0 and hch == 0:
                            w.append(s_ready)
                        last_pe = mm_group(emits, waits=w)
                        gs = (tsub * 2 + hch) % 2
                        act((lambda bank, gs: lambda e: e.activation(
                            out=gtmp[:, gs, :], in_=ps[:, bank, :], func=AF.Gelu))(bank, gs),
                            waits=[last_pe, (s_pl, max(0, s_pl.count - 1))])
                        ac_g = (s_ac, s_ac.count)
                        free["bank"][bank] = ac_g
                        w = [ac_g]
                        if dpass == 0 and tsub == 0 and hch == 0:
                            w.append((s_x, x_now))
                        pl.step("gpsimd", (lambda tsub, hch, gs: lambda e: e.tensor_tensor(
                            out=xbuf[:, tsub, hch * 512:(hch + 1) * 512],
                            in0=xbuf[:, tsub, hch * 512:(hch + 1) * 512],
                            in1=gtmp[:, gs, :], op=ALU.add))(tsub, hch, gs),
                            waits=w, incs=[(s_pl, 1)])
                if dpass == 0:
                    free["ssf"] = last_pe
                else:
                    free["sbb"] = last_pe
            pl_x1 = s_pl.count
            free["xnT"] = last_pe

            ln2_stats_dve(waits=[(s_pl, pl_x1)])
            dves(lambda e: e.reciprocal(out=stat2[:, 8:12], in_=stat2[:, 4:8]))
            dves(lambda e: e.tensor_tensor(out=stat2[:, 12:16], in0=stat2[:, 0:4],
                                           in1=stat2[:, 4:8], op=ALU.mult))
            dves(lambda e: e.tensor_tensor(out=stat2[:, 16:20], in0=stat2[:, 12:16],
                                           in1=stat2[:, 12:16], op=ALU.subtract))
            dves(lambda e: e.tensor_tensor(out=stat3[:, 12:16], in0=stat2[:, 4:8],
                                           in1=stat2[:, 4:8], op=ALU.mult))
            dves(lambda e: e.tensor_tensor(out=stat3[:, 12:16], in0=stat3[:, 12:16],
                                           in1=stat[:, 8:12], op=ALU.mult))
            newton_rsqrt(stat3[:, 12:16], stat2[:, 20:24])
            dves(lambda e: e.tensor_tensor(out=stat3[:, 0:4], in0=stat2[:, 16:20],
                                           in1=stat2[:, 8:12], op=ALU.mult))
            dves(lambda e: e.tensor_tensor(out=stat3[:, 0:4], in0=stat3[:, 0:4],
                                           in1=stat2[:, 0:4], op=ALU.add))
            dves(lambda e: e.tensor_tensor(out=stat3[:, 4:8], in0=stat2[:, 4:8],
                                           in1=stat2[:, 20:24], op=ALU.mult))
            dves(lambda e: e.tensor_scalar(out=stat3[:, 8:12], in0=stat2[:, 4:8],
                                           scalar1=1.0, scalar2=None, op0=ALU.add))

            for s in range(NSUB):
                hs = s % 2
                w = []
                if s >= 2:
                    w.append((s_pe, s_pe.count))
                dve((lambda s, hs: lambda e: e.tensor_scalar(
                    out=hbuf[:, hs, :], in0=xbuf[:, s, :],
                    scalar1=stat3[:, 0 + s:1 + s], scalar2=stat3[:, 4 + s:5 + s],
                    op0=ALU.subtract, op1=ALU.mult))(s, hs), waits=w)
                dve((lambda hs: lambda e: e.scalar_tensor_tensor(
                    out=hbuf[:, hs, :], in0=hbuf[:, hs, :], scalar=0.0,
                    in1=gamb[:, 0, :], op0=ALU.add, op1=ALU.mult))(hs))
                dve((lambda hs: lambda e: e.tensor_tensor(
                    out=hbuf[:, hs, :], in0=hbuf[:, hs, :], in1=gamb[:, 1, :],
                    op=ALU.add))(hs))
                dve((lambda s: lambda e: e.tensor_scalar(
                    out=xbuf[:, s, :], in0=xbuf[:, s, :],
                    scalar1=stat3[:, 8 + s:9 + s], scalar2=stat2[:, 12 + s:13 + s],
                    op0=ALU.mult, op1=ALU.subtract))(s))
                dv_h = s_dv.count
                pl.step("gpsimd", (lambda s: lambda e: e.tensor_tensor(
                    out=xbuf[:, s, :], in0=xbuf[:, s, :], in1=hb2[:], op=ALU.add))(s),
                    waits=[(s_dv, dv_h)], incs=[(s_pl, 1)])
                for g4 in range(2):
                    bank = 2 + g4
                    emits = []
                    for s4 in range(4):
                        ht_i = g4 * 4 + s4
                        emits.append((lambda hs, ht_i, bank, s4: lambda e: e.transpose(
                            ps[:, bank, s4 * 128:(s4 + 1) * 128],
                            hbuf[:, hs, ht_i * 128:(ht_i + 1) * 128], idf[:]))(hs, ht_i, bank, s4))
                    w = [free["bank"][bank]]
                    if g4 == 0:
                        w.append((s_dv, dv_h))
                    if s == 0 and g4 == 0:
                        w.append(free["hT"])
                    pe_t = mm_group(emits, waits=w)
                    act((lambda g4, s, bank: lambda e: e.activation(
                        out=hT[:, g4 * 4:(g4 + 1) * 4, s * 128:(s + 1) * 128],
                        in_=ps[:, bank, :].rearrange("p (s4 c) -> p s4 c", c=128),
                        func=AF.Copy))(g4, s, bank), waits=[pe_t])
                    free["bank"][bank] = (s_ac, s_ac.count)
            ac_hT = s_ac.count

            if j + 1 < NBLK:
                state[j + 1] = attn_pre(j + 1)

            w1_ready = {}

            def w1_load(ck):
                slot = ck % 2
                pl.step("sync", (lambda ck, slot: lambda e: e.dma_start(
                    out=w1b[:, slot, :, :, :],
                    in_=w1s[2 * ck:2 * ck + 2].rearrange("f2 p kt c -> p f2 kt c")))(ck, slot),
                    waits=[free["w1"][slot]], incs=[(s_w1, 16)])
                w1_ready[ck] = s_w1.count
            w1_load(0)
            w1_load(1)
            for ft in range(NFT):
                ck = ft // 2
                slot = ck % 2
                bank = 4 + ft % 2
                emits = []
                for kt in range(NHT):
                    emits.append((lambda slot, ft, kt, bank: lambda e: e.matmul(
                        ps[:, bank, :], w1b[:, slot, ft % 2, kt, :], hT[:, kt, :],
                        start=(kt == 0), stop=(kt == NHT - 1)))(slot, ft, kt, bank))
                w = [free["bank"][bank], (s_w1, w1_ready[ck])]
                if ft == 0:
                    w.append((s_ac, ac_hT))
                pe_f = mm_group(emits, waits=w)
                if ft % 2 == 1:
                    free["w1"][slot] = pe_f
                    if ck + 2 < NFT // 2:
                        w1_load(ck + 2)
                act((lambda ft, bank: lambda e: e.activation(
                    out=ybuf[:, ft, :], in_=ps[:, bank, :], func=AF.Silu,
                    bias=f1bias[:, ft:ft + 1]))(ft, bank), waits=[pe_f])
                free["bank"][bank] = (s_ac, s_ac.count)
            ac_y = s_ac.count

            pl_now = s_pl.count
            for hh in range(2):
                w2_ready = {}

                def w2_load(ck, hh=hh):
                    slot = ck % 2
                    pl.step("scalar", (lambda ck, slot, hh: lambda e: e.dma_start(
                        out=w2b[:, slot, :, 0:512],
                        in_=w2s[2 * ck:2 * ck + 2, :, hh * 512:(hh + 1) * 512].rearrange(
                            "f2 p h -> p f2 h")))(ck, slot, hh),
                        waits=[free["w2"][slot]], incs=[(s_w2, 16)])
                    w2_ready[ck] = s_w2.count
                w2_load(0)
                w2_load(1)
                pe_fc2 = None
                for ft in range(NFT):
                    ck = ft // 2
                    slot = ck % 2
                    for tsub in range(NSUB):
                        bank = 4 + tsub
                        w = []
                        if ft == 0:
                            w.append(free["bank"][bank])
                            if tsub == 0 and hh == 0:
                                w.append((s_ac, ac_y))
                        if tsub == 0:
                            w.append((s_w2, w2_ready[ck]))
                        pe_fc2 = mm_group([(lambda slot, ft, tsub, hh, bank: lambda e: e.matmul(
                            ps[:, bank, :], ybuf[:, ft, tsub * 128:(tsub + 1) * 128],
                            w2b[:, slot, ft % 2, 0:512],
                            start=(ft == 0), stop=(ft == NFT - 1),
                            skip_group_check=True))(slot, ft, tsub, hh, bank)], waits=w)
                    if ft % 2 == 1:
                        free["w2"][slot] = pe_fc2
                        if ck + 2 < NFT // 2:
                            w2_load(ck + 2)
                for tsub in range(NSUB):
                    bank = 4 + tsub
                    w = [pe_fc2]
                    if tsub == 0 and hh == 0:
                        w.append((s_pl, pl_now))
                        w.append(free["obuf"][0])
                    if hh == 1 and tsub == 0:
                        w.append((s_out, s_out.count))
                    dve((lambda tsub, hh, bank: lambda e: e.scalar_tensor_tensor(
                        out=obuf[:, tsub, :], in0=ps[:, bank, :],
                        scalar=0.5, in1=xbuf[:, tsub, hh * 512:(hh + 1) * 512],
                        op0=ALU.mult, op1=ALU.add))(tsub, hh, bank), waits=w)
                    free["bank"][bank] = (s_dv, s_dv.count)
                    spacer()
                    dv_o = s_dv.count
                    pl.step("gpsimd", (lambda: lambda e: e.tensor_tensor(
                        out=cst4[:, 8:12], in0=cst4[:, 0:4], in1=cst4[:, 4:8],
                        op=ALU.mult))(), waits=[(s_dv, dv_o)], incs=[(s_pl, 1)])
                    fence = s_pl.count
                    pl.step("gpsimd", (lambda b, j, tsub, hh: lambda e: e.dma_start(
                        out=y_out[b, j * TB + tsub * 128:j * TB + (tsub + 1) * 128,
                                  hh * 512:(hh + 1) * 512],
                        in_=obuf[:, tsub, :]))(b, j, tsub, hh),
                        waits=[(s_pl, fence)], incs=[(s_out, 16)])
            free["obuf"] = [(s_out, s_out.count)] * 2
            free["xbuf"] = (s_dv, s_dv.count)

    prep()
    free["xbuf"] = (s_dv, s_dv.count)
    for b in range(BPC):
        pass_a(b)
        pass_b(b)
    pl.step("gpsimd", lambda e: e.nop(), waits=[(s_out, s_out.count)])

    with es:
        with contextlib.ExitStack() as es2:
            for s in pl.sems:
                s.handle = es2.enter_context(nc.semaphore(s.name))
            block = es2.enter_context(nc.Block())
            for ename, attr in (("sync", "sync"), ("vector", "vector"),
                                ("scalar", "scalar"), ("tensor", "tensor"),
                                ("gpsimd", "gpsimd")):
                def make_section(ename):
                    def section(eng):
                        for (en, fn, waits, incs) in pl.steps:
                            if en != ename:
                                continue
                            for (sem, val) in waits:
                                eng.wait_ge(sem.handle, val)
                            inst = fn(eng)
                            for (sem, n) in incs:
                                inst.then_inc(sem.handle, n)
                    return section
                getattr(block, attr)(make_section(ename))
    return nc


def kernel(**inputs):
    xs = np.ascontiguousarray(inputs["L0_attn_x"])
    assert xs.shape == (16, L, H)
    bf = ml_dtypes.bfloat16
    common = {k: np.ascontiguousarray(v) for k, v in inputs.items() if k != "L0_attn_x"}
    common["c_idf"] = np.eye(128, dtype=np.float32)
    common["c_idb"] = np.eye(128).astype(bf)
    nc = build_program()
    in_maps = []
    for c in range(NCORES):
        m = dict(common)
        m["L0_attn_x"] = xs[c * BPC:(c + 1) * BPC]
        in_maps.append(m)
    res = run_bass_kernel_spmd(nc, in_maps, core_ids=list(range(NCORES)))
    return np.concatenate([r["out"] for r in res.results], axis=0)


# version 28
# speedup vs baseline: 1.0631x; 1.0631x over previous
import contextlib
import numpy as np
import ml_dtypes
import concourse.bass as bass
import concourse.mybir as mybir
from concourse.bass_utils import run_bass_kernel_spmd

F32 = mybir.dt.float32
BF16 = mybir.dt.bfloat16
I32 = mybir.dt.int32
ALU = mybir.AluOpType
AF = mybir.ActivationFunctionType
AX = mybir.AxisListType

NCORES = 8
BPC = 2
L = 4096
H = 1024
P = 256
TB = 512
NBLK = L // TB
NSUB = TB // 128
NHT = H // 128
FF = 4096
NFT = FF // 128
EPS = 1e-5
INV_H = 1.0 / H
TWO_PI_HI = 6.28125
TWO_PI_LO = 0.0019353071795864769
INV_2PI = float(1.0 / (2.0 * np.pi))
RSQRT_MAGIC = 0x5f3759df


class Sem:
    def __init__(self, name):
        self.name = name
        self.count = 0
        self.handle = None


class Plan:
    def __init__(self):
        self.steps = []
        self.sems = []

    def sem(self, name):
        s = Sem(name)
        self.sems.append(s)
        return s

    def step(self, engine, fn, waits=(), incs=()):
        ws = [(s, v) for (s, v) in waits if v > 0]
        incs = list(incs)
        for s, n in incs:
            s.count += n
        self.steps.append((engine, fn, ws, incs))


def build_program():
    nc = bass.Bass("TRN2", target_bir_lowering=False, debug=False)

    def din(name, shape, dt=F32):
        return nc.dram_tensor(name, shape, dt, kind="ExternalInput").ap()

    x_in = din("L0_attn_x", [BPC, L, H])
    prm = {}
    for d in ("fwd", "bwd"):
        for nm, shp in [("log_real", [P]), ("imag", [P]), ("B_real", [P, H]),
                        ("B_imag", [P, H]), ("C_real", [H, P]), ("C_imag", [H, P]),
                        ("D", [H]), ("log_Delta", [P])]:
            prm[f"{d}_{nm}"] = din(f"L0_attn_{d}_{nm}", shp)
    ln_g = din("L0_ffn_ln_gamma", [H])
    ln_b = din("L0_ffn_ln_beta", [H])
    fc1_q = din("L0_ffn_fc1_q", [H, FF], I32)
    fc1_sc = din("L0_ffn_fc1_scales", [H // 128, FF])
    fc1_b = din("L0_ffn_fc1_bias", [FF])
    fc2_q = din("L0_ffn_fc2_q", [FF, H], I32)
    fc2_sc = din("L0_ffn_fc2_scales", [FF // 128, H])
    fc2_b = din("L0_ffn_fc2_bias", [H])
    idf_in = din("c_idf", [128, 128])
    idb_in = din("c_idb", [128, 128], BF16)

    y_out = nc.dram_tensor("out", [BPC, L, H], F32, kind="ExternalOutput").ap()

    w1s = nc.dram_tensor("w1s", [NFT, 128, NHT, 128], BF16, kind="Internal").ap()
    w2s = nc.dram_tensor("w2s", [NFT, 128, H], BF16, kind="Internal").ap()
    xsb = nc.dram_tensor("xsb", [BPC, 2, 2, 128, L], BF16, kind="Internal").ap()
    xnt_s = nc.dram_tensor("xnt_s", [BPC, NBLK, 128, NHT, TB], BF16, kind="Internal").ap()

    es = contextlib.ExitStack()

    def sb(name, shape, dt=F32):
        return es.enter_context(nc.sbuf_tensor(name, shape, dt))

    xbuf = sb("xbuf", [128, NSUB, H])
    xnT = sb("xnT", [128, NHT, TB], BF16)
    hT = sb("hT", [128, NHT, TB], BF16)
    sbb = sb("sbb", [128, 4, TB], BF16)
    ssf = sb("ssf", [128, 4, TB], BF16)
    wpl = sb("wpl", [128, 4, TB])
    fpl = sb("fpl", [128, 4, TB])
    bpr = sb("bpr", [128, 4, TB])
    tmpa = sb("tmpa", [128, TB])
    tmpb = sb("tmpb", [128, TB])
    gtmp = sb("gtmp", [128, 2, TB])
    hbuf = sb("hbuf", [128, 2, H])
    ybuf = sb("ybuf", [128, NFT, TB], BF16)
    w1b = sb("w1b", [128, 2, 2, NHT, 128], BF16)
    w2b = sb("w2b", [128, 2, 2, 512], BF16)
    obuf = sb("obuf", [128, 4, 512])
    rbase = sb("rbase", [128, 2, 2, 2, TB])
    btst = sb("btst", [128, 2, NHT, 2, P], BF16)
    cmov = sb("cmov", [128, 2, 2, 2, H], BF16)
    ddg = sb("ddg", [128, 2, NHT, 128], BF16)
    gvec = sb("gvec", [128, NHT])
    bvec = sb("bvec", [128, NHT])
    hb2 = sb("hb2", [128, H])
    idf = sb("idf", [128, 128])
    idb = sb("idb", [128, 128], BF16)
    f1bias = sb("f1bias", [128, NFT])
    dvec = sb("dvec", [128, 2, NHT])
    mu1s = sb("mu1s", [128, NBLK, NSUB])
    r1s = sb("r1s", [128, NBLK, NSUB])
    stat = sb("stat", [128, 16])
    stat2 = sb("stat2", [128, 24])
    stat3 = sb("stat3", [128, 16])
    nst = sb("nst", [128, 8])
    cst4 = sb("cst4", [128, 12])
    svec = sb("svec", [128, 2, 2, 24])
    junk = sb("junk", [128, H])
    qrow = xbuf[:].rearrange("p s h -> p (s h)")
    qbf = ybuf[:, 0:8, :].rearrange("p a b -> p (a b)")
    scbc = rbase[:].rearrange("p a b c t -> p (a b c t)")

    ps = es.enter_context(nc.psum_tensor("ps", [128, 8, 512], F32))

    pl = Plan()
    s_cst = pl.sem("s_cst")
    s_x = pl.sem("s_x")
    s_xnt = pl.sem("s_xnt")
    s_sbb = pl.sem("s_sbb")
    s_w1 = pl.sem("s_w1")
    s_w2 = pl.sem("s_w2")
    s_q = pl.sem("s_q")
    s_scw = pl.sem("s_scw")
    s_out = pl.sem("s_out")
    s_pe = pl.sem("s_pe")
    s_dv = pl.sem("s_dv")
    s_ac = pl.sem("s_ac")
    s_pl = pl.sem("s_pl")
    s_w1b = pl.sem("s_w1b")

    free = {
        "bank": [(s_pe, 0)] * 8,
        "xbuf": (s_pe, 0), "xnT": (s_pe, 0), "hT": (s_pe, 0),
        "sbb": (s_pe, 0), "ssf": (s_pe, 0),
        "w1": [(s_pe, 0), (s_pe, 0)], "w2": [(s_pe, 0), (s_pe, 0)],
        "obuf": [(s_pe, 0), (s_pe, 0)],
    }

    def dve(fn, waits=()):
        pl.step("vector", fn, waits=waits, incs=[(s_dv, 1)])

    def spacer():
        dve(lambda e: e.tensor_tensor(out=cst4[:, 8:12], in0=cst4[:, 0:4],
                                      in1=cst4[:, 4:8], op=ALU.mult))

    def dves(fn, waits=()):
        dve(fn, waits=waits)
        spacer()

    def act(fn, waits=()):
        pl.step("scalar", fn, waits=waits, incs=[(s_ac, 1)])

    def mm_group(emits, waits=()):
        w = list(waits)
        for fn in emits[:-1]:
            pl.step("tensor", fn, waits=w)
            w = []
        pl.step("tensor", emits[-1], waits=w, incs=[(s_pe, 1)])
        return (s_pe, s_pe.count)

    def newton_rsqrt(v_ap, r_ap):
        v2 = nst[:, 0:4]
        t1 = nst[:, 4:8]
        dves((lambda v_ap: lambda e: e.scalar_tensor_tensor(
            out=v2, in0=v_ap, scalar=1.0, in1=cst4[:, 0:4],
            op0=ALU.mult, op1=ALU.add))(v_ap))
        dves((lambda r_ap: lambda e: e.tensor_scalar(
            out=r_ap.bitcast(I32), in0=v2.bitcast(I32),
            scalar1=1, scalar2=None, op0=ALU.arith_shift_right))(r_ap))
        dves((lambda r_ap: lambda e: e.tensor_scalar(
            out=r_ap.bitcast(I32), in0=r_ap.bitcast(I32),
            scalar1=-1, scalar2=RSQRT_MAGIC, op0=ALU.mult, op1=ALU.add))(r_ap))
        for _ in range(3):
            dves((lambda r_ap: lambda e: e.tensor_tensor(out=t1, in0=r_ap, in1=r_ap, op=ALU.mult))(r_ap))
            dves(lambda e: e.tensor_tensor(out=t1, in0=t1, in1=v2, op=ALU.mult))
            dves(lambda e: e.scalar_tensor_tensor(out=t1, in0=t1, scalar=-0.5,
                                                  in1=cst4[:, 4:8], op0=ALU.mult, op1=ALU.add))
            dves((lambda r_ap: lambda e: e.tensor_tensor(out=r_ap, in0=r_ap, in1=t1, op=ALU.mult))(r_ap))

    LOG2E = float(np.log2(np.e))
    LN2_HI = 0.6933594
    LN2_LO = -0.00021219444005469057

    def soft_exp(dst, srcp, waits=()):
        t = nst[:, 0:1]
        n = nst[:, 1:2]
        r = nst[:, 2:3]
        p = nst[:, 3:4]
        dves((lambda: lambda e: e.tensor_scalar(out=t, in0=srcp, scalar1=LOG2E,
                                                scalar2=None, op0=ALU.mult))(), waits=waits)
        dves(lambda e: e.tensor_copy(out=n.bitcast(I32), in_=t))
        dves(lambda e: e.tensor_copy(out=n, in_=n.bitcast(I32)))
        dves((lambda: lambda e: e.scalar_tensor_tensor(out=r, in0=n, scalar=-LN2_HI,
                                                       in1=srcp, op0=ALU.mult, op1=ALU.add))())
        dves(lambda e: e.scalar_tensor_tensor(out=r, in0=n, scalar=-LN2_LO,
                                              in1=r, op0=ALU.mult, op1=ALU.add))
        dves(lambda e: e.tensor_scalar(out=p, in0=r, scalar1=1.0 / 6, scalar2=1.0,
                                       op0=ALU.mult, op1=ALU.add))
        for dcf in (1.0 / 5, 1.0 / 4, 1.0 / 3, 1.0 / 2, 1.0):
            dves((lambda dcf: lambda e: e.tensor_tensor(out=p, in0=p, in1=r, op=ALU.mult))(dcf))
            dves((lambda dcf: lambda e: e.tensor_scalar(out=p, in0=p, scalar1=dcf,
                                                        scalar2=1.0, op0=ALU.mult, op1=ALU.add))(dcf))
        dves(lambda e: e.tensor_scalar(out=n, in0=n, scalar1=-126.0, scalar2=None,
                                       op0=ALU.max))
        dves(lambda e: e.tensor_scalar(out=n, in0=n, scalar1=127.0, scalar2=None,
                                       op0=ALU.min))
        dves(lambda e: e.tensor_copy(out=t.bitcast(I32), in_=n))
        dves(lambda e: e.tensor_scalar(out=t.bitcast(I32), in0=t.bitcast(I32),
                                       scalar1=127, scalar2=None, op0=ALU.add))
        dves(lambda e: e.tensor_scalar(out=t.bitcast(I32), in0=t.bitcast(I32),
                                       scalar1=23, scalar2=None, op0=ALU.logical_shift_left))
        dves((lambda: lambda e: e.tensor_tensor(out=dst, in0=p, in1=t, op=ALU.mult))())

    def stats_tail(mu_ap, r_ap, waits):
        dves((lambda mu_ap: lambda e: e.tensor_scalar(
            out=mu_ap, in0=stat[:, 0:4], scalar1=INV_H, scalar2=None,
            op0=ALU.mult))(mu_ap), waits=waits)
        dves((lambda mu_ap: lambda e: e.tensor_tensor(
            out=stat[:, 12:16], in0=mu_ap, in1=mu_ap, op=ALU.mult))(mu_ap))
        dves(lambda e: e.scalar_tensor_tensor(out=stat[:, 8:12], in0=stat[:, 4:8],
                                              scalar=INV_H, in1=stat[:, 12:16],
                                              op0=ALU.mult, op1=ALU.subtract))
        newton_rsqrt(stat[:, 8:12], r_ap)

    def prep():
        loads = [
            (idf[:], idf_in), (idb[:], idb_in),
            (hb2[:], fc2_b[None, :].partition_broadcast(128)),
        ]
        for dst, src in loads:
            pl.step("sync", (lambda d, s: (lambda e: e.dma_start(out=d, in_=s)))(dst, src),
                    incs=[(s_cst, 16)])
        for i in range(NFT):
            pl.step("sync", (lambda i: (lambda e: e.dma_start(
                out=f1bias[:, i:i + 1], in_=fc1_b[i * 128:(i + 1) * 128].unsqueeze(1))))(i),
                incs=[(s_cst, 16)])
        for i in range(NHT):
            pl.step("sync", (lambda i: (lambda e: e.dma_start(
                out=gvec[:, i:i + 1], in_=ln_g[i * 128:(i + 1) * 128].unsqueeze(1))))(i),
                incs=[(s_cst, 16)])
            pl.step("sync", (lambda i: (lambda e: e.dma_start(
                out=bvec[:, i:i + 1], in_=ln_b[i * 128:(i + 1) * 128].unsqueeze(1))))(i),
                incs=[(s_cst, 16)])
        for di, d in enumerate(("fwd", "bwd")):
            for i in range(NHT):
                pl.step("sync", (lambda d, di, i: (lambda e: e.dma_start(
                    out=dvec[:, di, i:i + 1],
                    in_=prm[f"{d}_D"][i * 128:(i + 1) * 128].unsqueeze(1))))(d, di, i),
                    incs=[(s_cst, 16)])
            for pb in range(2):
                for sl, nm in ((0, "log_real"), (1, "imag"), (2, "log_Delta")):
                    pl.step("sync", (lambda d, di, pb, sl, nm: (lambda e: e.dma_start(
                        out=svec[:, di, pb, sl:sl + 1],
                        in_=prm[f"{d}_{nm}"][pb * 128:(pb + 1) * 128].unsqueeze(1))))(d, di, pb, sl, nm),
                        incs=[(s_cst, 16)])
        cst_all = s_cst.count

        dve(lambda e: e.memset(cst4[:, 0:4], EPS))
        dve(lambda e: e.memset(cst4[:, 4:8], 1.5))
        dve(lambda e: e.memset(cst4[:, 8:12], 0.0))

        first = [True]
        for di in range(2):
            for pb in range(2):
                S = (lambda sl, di=di, pb=pb: svec[:, di, pb, sl:sl + 1])
                w0 = [(s_cst, cst_all)] if first[0] else []
                first[0] = False
                soft_exp(S(3), S(0), waits=w0)
                soft_exp(S(4), S(2))
        for di in range(2):
            for pb in range(2):
                S = (lambda sl, di=di, pb=pb: svec[:, di, pb, sl:sl + 1])
                w0 = []
                dves((lambda S: lambda e: e.scalar_tensor_tensor(
                    out=S(5), in0=S(3), scalar=-1.0, in1=S(4),
                    op0=ALU.mult, op1=ALU.mult))(S), waits=w0)
                dves((lambda S: lambda e: e.tensor_tensor(out=S(6), in0=S(1), in1=S(4), op=ALU.mult))(S))
                dves((lambda S: lambda e: e.tensor_scalar(out=S(22), in0=S(6), scalar1=INV_2PI,
                                                          scalar2=None, op0=ALU.mult))(S))
                dves((lambda S: lambda e: e.tensor_copy(out=S(8).bitcast(I32), in_=S(22)))(S))
                dves((lambda S: lambda e: e.tensor_copy(out=S(8), in_=S(8).bitcast(I32)))(S))
                dves((lambda S: lambda e: e.scalar_tensor_tensor(
                    out=S(23), in0=S(8), scalar=-TWO_PI_HI, in1=S(6),
                    op0=ALU.mult, op1=ALU.add))(S))
                dves((lambda S: lambda e: e.scalar_tensor_tensor(
                    out=S(9), in0=S(8), scalar=-TWO_PI_LO, in1=S(23),
                    op0=ALU.mult, op1=ALU.add))(S))
                dves((lambda S: lambda e: e.tensor_scalar(out=S(22), in0=S(9), scalar1=0.25,
                                                          scalar2=None, op0=ALU.mult))(S))
                dves((lambda S: lambda e: e.tensor_tensor(out=S(23), in0=S(22), in1=S(22), op=ALU.mult))(S))
                for k, c in enumerate([1.0 / 362880, -1.0 / 5040, 1.0 / 120, -1.0 / 6]):
                    if k == 0:
                        dves((lambda S, c=c: lambda e: e.tensor_scalar(
                            out=S(10), in0=S(23), scalar1=c, scalar2=None, op0=ALU.mult))(S))
                    else:
                        dves((lambda S, c=c: lambda e: e.tensor_scalar(
                            out=S(10), in0=S(10), scalar1=c, scalar2=None, op0=ALU.add))(S))
                        dves((lambda S: lambda e: e.tensor_tensor(
                            out=S(10), in0=S(10), in1=S(23), op=ALU.mult))(S))
                dves((lambda S: lambda e: e.tensor_scalar(out=S(10), in0=S(10), scalar1=1.0,
                                                          scalar2=None, op0=ALU.add))(S))
                dves((lambda S: lambda e: e.tensor_tensor(out=S(10), in0=S(10), in1=S(22), op=ALU.mult))(S))
                for k, c in enumerate([1.0 / 40320, -1.0 / 720, 1.0 / 24, -0.5]):
                    if k == 0:
                        dves((lambda S, c=c: lambda e: e.tensor_scalar(
                            out=S(11), in0=S(23), scalar1=c, scalar2=None, op0=ALU.mult))(S))
                    else:
                        dves((lambda S, c=c: lambda e: e.tensor_scalar(
                            out=S(11), in0=S(11), scalar1=c, scalar2=None, op0=ALU.add))(S))
                        dves((lambda S: lambda e: e.tensor_tensor(
                            out=S(11), in0=S(11), in1=S(23), op=ALU.mult))(S))
                dves((lambda S: lambda e: e.tensor_scalar(out=S(11), in0=S(11), scalar1=1.0,
                                                          scalar2=None, op0=ALU.add))(S))
                for _ in range(2):
                    dves((lambda S: lambda e: e.tensor_tensor(out=S(22), in0=S(11), in1=S(11), op=ALU.mult))(S))
                    dves((lambda S: lambda e: e.tensor_tensor(out=S(23), in0=S(10), in1=S(10), op=ALU.mult))(S))
                    dves((lambda S: lambda e: e.tensor_tensor(out=S(10), in0=S(10), in1=S(11), op=ALU.mult))(S))
                    dves((lambda S: lambda e: e.tensor_scalar(out=S(10), in0=S(10), scalar1=2.0,
                                                              scalar2=None, op0=ALU.mult))(S))
                    dves((lambda S: lambda e: e.tensor_tensor(out=S(11), in0=S(22), in1=S(23), op=ALU.subtract))(S))
        for di in range(2):
            for pb in range(2):
                S = (lambda sl, di=di, pb=pb: svec[:, di, pb, sl:sl + 1])
                soft_exp(S(7), S(5))
        for di in range(2):
            for pb in range(2):
                S = (lambda sl, di=di, pb=pb: svec[:, di, pb, sl:sl + 1])
                dves((lambda S: lambda e: e.tensor_tensor(out=S(12), in0=S(7), in1=S(11), op=ALU.mult))(S))
                dves((lambda S: lambda e: e.tensor_tensor(out=S(13), in0=S(7), in1=S(10), op=ALU.mult))(S))
                dves((lambda S: lambda e: e.tensor_copy(out=S(16), in_=S(11)))(S))
                dves((lambda S: lambda e: e.tensor_scalar(out=S(17), in0=S(10), scalar1=-1.0,
                                                          scalar2=None, op0=ALU.mult))(S))
                dves((lambda S: lambda e: e.tensor_tensor(out=S(22), in0=S(1), in1=S(1), op=ALU.mult))(S))
                dves((lambda S: lambda e: e.tensor_tensor(out=S(23), in0=S(3), in1=S(3), op=ALU.mult))(S))
                dves((lambda S: lambda e: e.tensor_tensor(out=S(22), in0=S(22), in1=S(23), op=ALU.add))(S))
                dves((lambda S: lambda e: e.tensor_scalar(out=S(23), in0=S(22), scalar1=1e-12,
                                                          scalar2=None, op0=ALU.is_lt))(S))
                dves((lambda S: lambda e: e.tensor_scalar(out=S(21), in0=S(23), scalar1=-1.0,
                                                          scalar2=1.0, op0=ALU.mult, op1=ALU.add))(S))
                dves((lambda S: lambda e: e.tensor_scalar(out=S(18), in0=S(3), scalar1=-1.0,
                                                          scalar2=None, op0=ALU.mult))(S))
                dves((lambda S: lambda e: e.tensor_tensor(out=S(18), in0=S(18), in1=S(21), op=ALU.mult))(S))
                dves((lambda S: lambda e: e.tensor_scalar(out=S(20), in0=S(23), scalar1=1e-6,
                                                          scalar2=None, op0=ALU.mult))(S))
                dves((lambda S: lambda e: e.tensor_tensor(out=S(18), in0=S(18), in1=S(20), op=ALU.add))(S))
                dves((lambda S: lambda e: e.tensor_tensor(out=S(19), in0=S(21), in1=S(1), op=ALU.mult))(S))
                dves((lambda S: lambda e: e.tensor_tensor(out=S(22), in0=S(18), in1=S(18), op=ALU.mult))(S))
                dves((lambda S: lambda e: e.tensor_tensor(out=S(23), in0=S(19), in1=S(19), op=ALU.mult))(S))
                dves((lambda S: lambda e: e.tensor_tensor(out=S(22), in0=S(22), in1=S(23), op=ALU.add))(S))
                dves((lambda S: lambda e: e.reciprocal(out=S(22), in_=S(22)))(S))
                dves((lambda S: lambda e: e.tensor_scalar(out=S(23), in0=S(12), scalar1=-1.0,
                                                          scalar2=None, op0=ALU.add))(S))
                dves((lambda S: lambda e: e.tensor_tensor(out=S(14), in0=S(23), in1=S(18), op=ALU.mult))(S))
                dves((lambda S: lambda e: e.tensor_tensor(out=S(20), in0=S(13), in1=S(19), op=ALU.mult))(S))
                dves((lambda S: lambda e: e.tensor_tensor(out=S(14), in0=S(14), in1=S(20), op=ALU.add))(S))
                dves((lambda S: lambda e: e.tensor_tensor(out=S(14), in0=S(14), in1=S(22), op=ALU.mult))(S))
                dves((lambda S: lambda e: e.tensor_tensor(out=S(15), in0=S(13), in1=S(18), op=ALU.mult))(S))
                dves((lambda S: lambda e: e.tensor_tensor(out=S(20), in0=S(23), in1=S(19), op=ALU.mult))(S))
                dves((lambda S: lambda e: e.tensor_tensor(out=S(15), in0=S(15), in1=S(20), op=ALU.subtract))(S))
                dves((lambda S: lambda e: e.tensor_tensor(out=S(15), in0=S(15), in1=S(22), op=ALU.mult))(S))

        for di in range(2):
            for i in range(NHT):
                dve((lambda di, i: lambda e: e.tensor_scalar(
                    out=ddg[:, di, i, :], in0=idb[:], scalar1=dvec[:, di, i:i + 1],
                    scalar2=None, op0=ALU.mult))(di, i))
        dve(lambda e: e.tensor_scalar(out=hb2[:], in0=hb2[:], scalar1=0.5, scalar2=None, op0=ALU.mult))
        spacer()

        for di, d in enumerate(("fwd", "bwd")):
            for pb in range(2):
                S = (lambda sl, di=di, pb=pb: svec[:, di, pb, sl:sl + 1])
                pl.step("sync", (lambda d, pb: lambda e: e.dma_start(
                    out=qrow[:, 0:H], in_=prm[f"{d}_B_real"][pb * 128:(pb + 1) * 128, :]))(d, pb),
                    waits=[(s_dv, s_dv.count), (s_pe, s_pe.count)], incs=[(s_cst, 16)])
                pl.step("sync", (lambda d, pb: lambda e: e.dma_start(
                    out=qrow[:, H:2 * H], in_=prm[f"{d}_B_imag"][pb * 128:(pb + 1) * 128, :]))(d, pb),
                    incs=[(s_cst, 16)])
                c_now = s_cst.count
                pe_prev = s_pe.count
                dve((lambda S: lambda e: e.tensor_scalar(
                    out=scbc[:, 0:H], in0=qrow[:, 0:H], scalar1=S(14), scalar2=None,
                    op0=ALU.mult))(S), waits=[(s_cst, c_now), (s_pe, pe_prev)])
                dve((lambda S: lambda e: e.tensor_scalar(
                    out=scbc[:, H:2 * H], in0=qrow[:, H:2 * H], scalar1=S(15), scalar2=None,
                    op0=ALU.mult))(S))
                dve(lambda e: e.tensor_tensor(out=scbc[:, 0:H], in0=scbc[:, 0:H],
                                              in1=scbc[:, H:2 * H], op=ALU.subtract))
                dve((lambda S: lambda e: e.tensor_scalar(
                    out=scbc[:, H:2 * H], in0=qrow[:, H:2 * H], scalar1=S(14), scalar2=None,
                    op0=ALU.mult))(S))
                dve((lambda S: lambda e: e.tensor_scalar(
                    out=scbc[:, 2 * H:3 * H], in0=qrow[:, 0:H], scalar1=S(15), scalar2=None,
                    op0=ALU.mult))(S))
                dve(lambda e: e.tensor_tensor(out=scbc[:, H:2 * H], in0=scbc[:, H:2 * H],
                                              in1=scbc[:, 2 * H:3 * H], op=ALU.add))
                dvn = s_dv.count
                for comp in range(2):
                    for kt in range(NHT):
                        bank = kt % 2
                        w = [(s_dv, dvn)] if (comp == 0 and kt == 0) else []
                        w.append(free["bank"][bank])
                        pe_t = mm_group([(lambda comp, kt, bank: lambda e: e.transpose(
                            ps[:, bank, 0:128],
                            scbc[:, comp * H + kt * 128:comp * H + (kt + 1) * 128],
                            idf[:]))(comp, kt, bank)], waits=w)
                        act((lambda di, pb, comp, kt, bank: lambda e: e.activation(
                            out=btst[:, di, kt, comp, pb * 128:(pb + 1) * 128],
                            in_=ps[:, bank, 0:128], func=AF.Copy))(di, pb, comp, kt, bank),
                            waits=[pe_t])
                        free["bank"][bank] = (s_ac, s_ac.count)

        for di, d in enumerate(("fwd", "bwd")):
            for comp, nm in ((0, "C_real"), (1, "C_imag")):
                for ht in range(NHT):
                    pl.step("sync", (lambda d, nm, ht: lambda e: e.dma_start(
                        out=qrow[:, 0:P], in_=prm[f"{d}_{nm}"][ht * 128:(ht + 1) * 128, :]))(d, nm, ht),
                        waits=[(s_pe, s_pe.count)], incs=[(s_cst, 16)])
                    c_now = s_cst.count
                    for pb in range(2):
                        bank = 2 + pb
                        w = [(s_cst, c_now), free["bank"][bank]]
                        pe_t = mm_group([(lambda pb, bank: lambda e: e.transpose(
                            ps[:, bank, 0:128], qrow[:, pb * 128:(pb + 1) * 128], idf[:]))(pb, bank)],
                            waits=w)
                        sgn = -1.0 if comp == 1 else 1.0
                        act((lambda di, pb, comp, ht, bank, sgn: lambda e: e.activation(
                            out=cmov[:, di, pb, comp, ht * 128:(ht + 1) * 128],
                            in_=ps[:, bank, 0:128], func=AF.Copy, scale=sgn))(di, pb, comp, ht, bank, sgn),
                            waits=[pe_t])
                        free["bank"][bank] = (s_ac, s_ac.count)

        for kt_idx, kt in enumerate(list(range(NHT)) + [0]):
            pl.step("gpsimd", (lambda kt: lambda e: e.dma_start(
                out=qrow[:], in_=fc1_q[kt * 128:(kt + 1) * 128, :]))(kt),
                waits=[(s_dv, s_dv.count), (s_scw, s_scw.count)], incs=[(s_q, 16)])
            pl.step("sync", (lambda kt: lambda e: e.dma_start(
                out=scbc[:, 0:FF],
                in_=fc1_sc[kt, :][None, :].partition_broadcast(128)))(kt),
                waits=[(s_dv, s_dv.count), (s_pe, s_pe.count)], incs=[(s_cst, 16)])
            dve(lambda e: e.scalar_tensor_tensor(out=qrow[:], in0=qrow[:], scalar=-128.0,
                                                 in1=scbc[:, 0:FF], op0=ALU.add, op1=ALU.mult),
                waits=[(s_q, s_q.count), (s_cst, s_cst.count)])
            spacer()
            dve((lambda kt: lambda e: e.tensor_scalar(
                out=qbf[:], in0=qrow[:], scalar1=gvec[:, kt:kt + 1], scalar2=None,
                op0=ALU.mult))(kt))
            spacer()
            spacer()
            for ft in range(NFT):
                pl.step("gpsimd", (lambda kt, ft: lambda e: e.dma_start(
                    out=w1s[ft, :, kt, :], in_=qbf[:, ft * 128:(ft + 1) * 128]))(kt, ft),
                    waits=[(s_dv, s_dv.count)] if ft == 0 else [], incs=[(s_scw, 16)])
            if kt_idx >= 1:
                dvq = s_dv.count
                for ft in range(NFT):
                    bank = 6
                    w = []
                    if ft == 0:
                        w.append((s_dv, dvq))
                        if kt_idx == 1:
                            w.append(free["bank"][bank])
                    pe_bw = mm_group([(lambda kt, ft, bank, kt_idx: lambda e: e.matmul(
                        ps[:, bank, ft:ft + 1], qrow[:, ft * 128:(ft + 1) * 128],
                        bvec[:, kt:kt + 1], start=(kt_idx == 1 and ft == 0),
                        stop=(kt_idx == NHT and ft == NFT - 1),
                        skip_group_check=True))(kt, ft, bank, kt_idx)], waits=w)
                if kt_idx == NHT:
                    dve(lambda e: e.tensor_tensor(out=f1bias[:, 0:NFT], in0=ps[:, 6, 0:NFT],
                                                  in1=f1bias[:, 0:NFT], op=ALU.add),
                        waits=[pe_bw])
                    spacer()
                    free["bank"][6] = (s_dv, s_dv.count)
        for kt in range(NFT):
            pl.step("gpsimd", (lambda kt: lambda e: e.dma_start(
                out=qrow[:, 0:H], in_=fc2_q[kt * 128:(kt + 1) * 128, :]))(kt),
                waits=[(s_dv, s_dv.count), (s_scw, s_scw.count)], incs=[(s_q, 16)])
            pl.step("sync", (lambda kt: lambda e: e.dma_start(
                out=scbc[:, 0:H], in_=fc2_sc[kt, :][None, :].partition_broadcast(128)))(kt),
                waits=[(s_dv, s_dv.count)], incs=[(s_cst, 16)])
            dve(lambda e: e.scalar_tensor_tensor(out=qbf[:, 0:H], in0=qrow[:, 0:H], scalar=-128.0,
                                                 in1=scbc[:, 0:H], op0=ALU.add, op1=ALU.mult),
                waits=[(s_q, s_q.count), (s_cst, s_cst.count)])
            spacer()
            spacer()
            pl.step("gpsimd", (lambda kt: lambda e: e.dma_start(
                out=w2s[kt], in_=qbf[:, 0:H]))(kt),
                waits=[(s_dv, s_dv.count)], incs=[(s_scw, 16)])

        for di in range(2):
            for pb in range(2):
                S = (lambda sl, di=di, pb=pb: svec[:, di, pb, sl:sl + 1])
                rr = rbase[:, di, pb, 0, :]
                ri = rbase[:, di, pb, 1, :]
                w0 = [(s_cst, s_cst.count), (s_q, s_q.count), (s_scw, s_scw.count)] \
                    if (di == 0 and pb == 0) else []
                dves((lambda rr=rr: lambda e: e.memset(rr[:, 0:1], 1.0))(), waits=w0)
                dves((lambda ri=ri: lambda e: e.memset(ri[:, 0:1], 0.0))())
                dves((lambda S: lambda e: e.tensor_copy(out=S(20), in_=S(16)))(S))
                dves((lambda S: lambda e: e.tensor_copy(out=S(21), in_=S(17)))(S))
                m = 1
                while m < TB:
                    dves((lambda rr, S, m: lambda e: e.tensor_scalar(
                        out=tmpa[:, 0:m], in0=rr[:, 0:m], scalar1=S(20), scalar2=None,
                        op0=ALU.mult))(rr, S, m))
                    dves((lambda ri, S, m: lambda e: e.tensor_scalar(
                        out=tmpb[:, 0:m], in0=ri[:, 0:m], scalar1=S(21), scalar2=None,
                        op0=ALU.mult))(ri, S, m))
                    dves((lambda rr, m: lambda e: e.tensor_tensor(
                        out=rr[:, m:2 * m], in0=tmpa[:, 0:m], in1=tmpb[:, 0:m],
                        op=ALU.subtract))(rr, m))
                    dves((lambda rr, S, m: lambda e: e.tensor_scalar(
                        out=tmpa[:, 0:m], in0=rr[:, 0:m], scalar1=S(21), scalar2=None,
                        op0=ALU.mult))(rr, S, m))
                    dves((lambda ri, S, m: lambda e: e.tensor_scalar(
                        out=tmpb[:, 0:m], in0=ri[:, 0:m], scalar1=S(20), scalar2=None,
                        op0=ALU.mult))(ri, S, m))
                    dves((lambda ri, m: lambda e: e.tensor_tensor(
                        out=ri[:, m:2 * m], in0=tmpa[:, 0:m], in1=tmpb[:, 0:m],
                        op=ALU.add))(ri, m))
                    if 2 * m < TB:
                        dves((lambda S: lambda e: e.tensor_tensor(out=S(22), in0=S(20), in1=S(20), op=ALU.mult))(S))
                        dves((lambda S: lambda e: e.tensor_tensor(out=S(23), in0=S(21), in1=S(21), op=ALU.mult))(S))
                        dves((lambda S: lambda e: e.tensor_tensor(out=S(21), in0=S(21), in1=S(20), op=ALU.mult))(S))
                        dves((lambda S: lambda e: e.tensor_tensor(out=S(20), in0=S(22), in1=S(23), op=ALU.subtract))(S))
                        dves((lambda S: lambda e: e.tensor_scalar(out=S(21), in0=S(21), scalar1=2.0,
                                                                  scalar2=None, op0=ALU.mult))(S))
                    m *= 2
                dves((lambda rr, S: lambda e: e.tensor_copy(out=S(22), in_=rr[:, TB - 1:TB]))(rr, S))
                dves((lambda ri, S: lambda e: e.tensor_copy(out=S(23), in_=ri[:, TB - 1:TB]))(ri, S))
                dves((lambda S: lambda e: e.tensor_tensor(out=S(18), in0=S(22), in1=S(16), op=ALU.mult))(S))
                dves((lambda S: lambda e: e.tensor_tensor(out=S(20), in0=S(23), in1=S(17), op=ALU.mult))(S))
                dves((lambda S: lambda e: e.tensor_tensor(out=S(18), in0=S(18), in1=S(20), op=ALU.subtract))(S))
                dves((lambda S: lambda e: e.tensor_tensor(out=S(19), in0=S(22), in1=S(17), op=ALU.mult))(S))
                dves((lambda S: lambda e: e.tensor_tensor(out=S(20), in0=S(23), in1=S(16), op=ALU.mult))(S))
                dves((lambda S: lambda e: e.tensor_tensor(out=S(19), in0=S(19), in1=S(20), op=ALU.add))(S))

    def ln1_stats_act(j, waits):
        for s in range(NSUB):
            act((lambda s: lambda e: e.activation(
                out=junk[:], in_=xbuf[:, s, :], func=AF.Copy,
                accum_out=stat[:, s:s + 1]))(s), waits=(waits if s == 0 else []))
            act((lambda s: lambda e: e.activation(
                out=junk[:], in_=xbuf[:, s, :], func=AF.Square,
                accum_out=stat[:, 4 + s:5 + s]))(s))
        a_now = s_ac.count
        stats_tail(mu1s[:, j, :], r1s[:, j, :], waits=[(s_ac, a_now)])
        return (s_dv, s_dv.count)

    def ln2_stats_dve(waits):
        for s in range(NSUB):
            dve((lambda s: lambda e: e.tensor_reduce(
                out=stat[:, s:s + 1], in_=xbuf[:, s, :], axis=AX.X, op=ALU.add))(s),
                waits=(waits if s == 0 else []))
            dve((lambda s: lambda e: e.scalar_tensor_tensor(
                out=junk[:], in0=xbuf[:, s, :], scalar=0.0, in1=xbuf[:, s, :],
                op0=ALU.add, op1=ALU.mult, accum_out=stat[:, 4 + s:5 + s]))(s))
        stats_tail(stat2[:, 0:4], stat2[:, 4:8], waits=[])

    def pow_init(di):
        for pb in range(2):
            S = (lambda sl, pb=pb: svec[:, di, pb, sl:sl + 1])
            dves((lambda S: lambda e: e.memset(S(20), 1.0))(S))
            dves((lambda S: lambda e: e.memset(S(21), 0.0))(S))

    def pow_update(di):
        for pb in range(2):
            S = (lambda sl, pb=pb: svec[:, di, pb, sl:sl + 1])
            dves((lambda S: lambda e: e.tensor_tensor(out=S(22), in0=S(20), in1=S(18), op=ALU.mult))(S))
            dves((lambda S: lambda e: e.tensor_tensor(out=S(23), in0=S(21), in1=S(19), op=ALU.mult))(S))
            dves((lambda S: lambda e: e.tensor_tensor(out=S(21), in0=S(21), in1=S(18), op=ALU.mult))(S))
            dves((lambda S: lambda e: e.tensor_tensor(out=S(22), in0=S(22), in1=S(23), op=ALU.subtract))(S))
            dves((lambda S: lambda e: e.tensor_tensor(out=S(23), in0=S(20), in1=S(19), op=ALU.mult))(S))
            dves((lambda S: lambda e: e.tensor_tensor(out=S(21), in0=S(21), in1=S(23), op=ALU.add))(S))
            dves((lambda S: lambda e: e.tensor_copy(out=S(20), in_=S(22)))(S))

    def build_F(di):
        for pb in range(2):
            S = (lambda sl, pb=pb: svec[:, di, pb, sl:sl + 1])
            rr = rbase[:, di, pb, 0, :]
            ri = rbase[:, di, pb, 1, :]
            fr = fpl[:, pb * 2 + 0, :]
            fi = fpl[:, pb * 2 + 1, :]
            dve((lambda S, rr: lambda e: e.tensor_scalar(
                out=tmpa[:], in0=rr, scalar1=S(20), scalar2=None, op0=ALU.mult))(S, rr))
            dve((lambda S, ri: lambda e: e.tensor_scalar(
                out=tmpb[:], in0=ri, scalar1=S(21), scalar2=None, op0=ALU.mult))(S, ri))
            dve((lambda fr: lambda e: e.tensor_tensor(
                out=fr, in0=tmpa[:], in1=tmpb[:], op=ALU.subtract))(fr))
            dve((lambda S, rr: lambda e: e.tensor_scalar(
                out=tmpa[:], in0=rr, scalar1=S(21), scalar2=None, op0=ALU.mult))(S, rr))
            dve((lambda S, ri: lambda e: e.tensor_scalar(
                out=tmpb[:], in0=ri, scalar1=S(20), scalar2=None, op0=ALU.mult))(S, ri))
            dve((lambda fi: lambda e: e.tensor_tensor(
                out=fi, in0=tmpa[:], in1=tmpb[:], op=ALU.add))(fi))

    def rotate_scan_unrotate(di, first_block, store_rev, pe_bu, s_free):
        for pb in range(2):
            fr = fpl[:, pb * 2 + 0, :]
            fi = fpl[:, pb * 2 + 1, :]
            br = ps[:, pb * 2 + 0, :]
            bi = ps[:, pb * 2 + 1, :]
            w0 = [pe_bu] if pb == 0 else []
            dve((lambda fr, br: lambda e: e.tensor_tensor(out=tmpa[:], in0=fr, in1=br, op=ALU.mult))(fr, br), waits=w0)
            dve((lambda fi, bi: lambda e: e.tensor_tensor(out=tmpb[:], in0=fi, in1=bi, op=ALU.mult))(fi, bi))
            dve((lambda pb: lambda e: e.tensor_tensor(
                out=bpr[:, pb * 2 + 0, :], in0=tmpa[:], in1=tmpb[:], op=ALU.subtract))(pb))
            dve((lambda fr, bi: lambda e: e.tensor_tensor(out=tmpa[:], in0=fr, in1=bi, op=ALU.mult))(fr, bi))
            dve((lambda fi, br: lambda e: e.tensor_tensor(out=tmpb[:], in0=fi, in1=br, op=ALU.mult))(fi, br))
            dve((lambda pb: lambda e: e.tensor_tensor(
                out=bpr[:, pb * 2 + 1, :], in0=tmpa[:], in1=tmpb[:], op=ALU.add))(pb))
        dv_rot = (s_dv, s_dv.count)
        for pc in range(4):
            pb = pc // 2
            rho = svec[:, di, pb, 7:8].broadcast_to([128, TB])
            if first_block:
                dve((lambda pc, rho: lambda e: e.tensor_tensor_scan(
                    out=wpl[:, pc, :], data0=rho, data1=bpr[:, pc, :], initial=0.0,
                    op0=ALU.mult, op1=ALU.add))(pc, rho))
            else:
                dve((lambda pc, rho: lambda e: e.tensor_tensor_scan(
                    out=wpl[:, pc, :], data0=rho, data1=bpr[:, pc, :],
                    initial=wpl[:, pc, TB - 1:TB], op0=ALU.mult, op1=ALU.add))(pc, rho))
        for pb in range(2):
            fr = fpl[:, pb * 2 + 0, :]
            fi = fpl[:, pb * 2 + 1, :]
            wr = wpl[:, pb * 2 + 0, :]
            wi = wpl[:, pb * 2 + 1, :]
            for comp in range(2):
                e1, e2, op = (wr, wi, ALU.add) if comp == 0 else (wi, wr, ALU.subtract)
                dst = ssf[:, pb * 2 + comp, ::-1] if store_rev else ssf[:, pb * 2 + comp, :]
                w0 = [s_free] if (pb == 0 and comp == 0) else []
                dve((lambda fr, e1: lambda e: e.tensor_tensor(out=tmpa[:], in0=fr, in1=e1, op=ALU.mult))(fr, e1), waits=w0)
                dve((lambda fi, e2: lambda e: e.tensor_tensor(out=tmpb[:], in0=fi, in1=e2, op=ALU.mult))(fi, e2))
                dve((lambda dst, op: lambda e: e.tensor_tensor(out=dst, in0=tmpa[:], in1=tmpb[:], op=op))(dst, op))
        return dv_rot, (s_dv, s_dv.count)

    def pass_a(b):
        pow_init(1)
        for J in range(NBLK):
            j = NBLK - 1 - J
            pl.step("sync", (lambda b, j: lambda e: e.dma_start(
                out=xbuf[:], in_=x_in[b, j * TB:(j + 1) * TB, :].rearrange(
                    "(s p) h -> p s h", p=128)))(b, j),
                waits=[free["xbuf"]], incs=[(s_x, 16)])
            x_now = s_x.count
            ln1_stats_act(j, waits=[(s_x, x_now)])
            for s in range(NSUB):
                hs = s % 2
                w = []
                if s >= 2:
                    w.append((s_pe, s_pe.count))
                dve((lambda s, hs, j: lambda e: e.tensor_scalar(
                    out=hbuf[:, hs, :], in0=xbuf[:, s, :],
                    scalar1=mu1s[:, j, s:s + 1], scalar2=r1s[:, j, s:s + 1],
                    op0=ALU.subtract, op1=ALU.mult))(s, hs, j), waits=w)
                dv_xn = s_dv.count
                for g4 in range(2):
                    bank = 4 + g4
                    emits = []
                    for s4 in range(4):
                        ht = g4 * 4 + s4
                        emits.append((lambda hs, ht, bank, s4: lambda e: e.transpose(
                            ps[:, bank, s4 * 128:(s4 + 1) * 128],
                            hbuf[:, hs, ht * 128:(ht + 1) * 128], idf[:]))(hs, ht, bank, s4))
                    w = [free["bank"][bank]]
                    if g4 == 0:
                        w.append((s_dv, dv_xn))
                    pe_t = mm_group(emits, waits=w)
                    w = [pe_t]
                    if s == 0 and g4 == 0:
                        w += [free["xnT"], free["hT"]]
                    act((lambda g4, s, bank: lambda e: e.activation(
                        out=xnT[:, g4 * 4:(g4 + 1) * 4, s * 128:(s + 1) * 128],
                        in_=ps[:, bank, :].rearrange("p (s4 c) -> p s4 c", c=128),
                        func=AF.Copy))(g4, s, bank), waits=w)
                    act((lambda g4, s, bank: lambda e: e.activation(
                        out=hT[:, g4 * 4:(g4 + 1) * 4, (NSUB - 1 - s) * 128:(NSUB - s) * 128],
                        in_=ps[:, bank, :].rearrange("p (s4 c) -> p s4 c", c=128)[:, :, ::-1],
                        func=AF.Copy))(g4, s, bank))
                    free["bank"][bank] = (s_ac, s_ac.count)
            ac_evac = s_ac.count
            pl.step("gpsimd", (lambda b, j: lambda e: e.dma_start(
                out=xnt_s[b, j], in_=xnT[:]))(b, j),
                waits=[(s_ac, ac_evac)], incs=[(s_scw, 16)])
            scw_xnt = s_scw.count
            free["xnT"] = (s_scw, scw_xnt)
            pe_bu = None
            for pb in range(2):
                for comp in range(2):
                    bank = pb * 2 + comp
                    emits = []
                    for kt in range(NHT):
                        emits.append((lambda pb, comp, kt, bank: lambda e: e.matmul(
                            ps[:, bank, :], btst[:, 1, kt, comp, pb * 128:(pb + 1) * 128],
                            hT[:, kt, :], start=(kt == 0), stop=(kt == NHT - 1)))(pb, comp, kt, bank))
                    w = [free["bank"][bank]]
                    if pb == 0 and comp == 0:
                        w += [(s_ac, ac_evac), (s_scw, scw_xnt)]
                    pe_bu = mm_group(emits, waits=w)
            free["hT"] = pe_bu
            build_F(1)
            dv_rot, dv_unrot = rotate_scan_unrotate(1, first_block=(J == 0),
                                                    store_rev=True, pe_bu=pe_bu,
                                                    s_free=free["ssf"])
            for k in range(4):
                free["bank"][k] = dv_rot
            for pc in range(4):
                pb, comp = pc // 2, pc % 2
                pl.step("gpsimd", (lambda b, pb, comp, j, pc: lambda e: e.dma_start(
                    out=xsb[b, pb, comp, :, j * TB:(j + 1) * TB], in_=ssf[:, pc, :]))(b, pb, comp, j, pc),
                    waits=[dv_unrot] if pc == 0 else [], incs=[(s_scw, 16)])
            free["ssf"] = (s_scw, s_scw.count)
            free["xbuf"] = dv_rot
            pow_update(1)

    def pass_b(b):
        pow_init(0)

        def load_x(j):
            pl.step("sync", (lambda b, j: lambda e: e.dma_start(
                out=xbuf[:], in_=x_in[b, j * TB:(j + 1) * TB, :].rearrange(
                    "(s p) h -> p s h", p=128)))(b, j),
                waits=[free["xbuf"]], incs=[(s_x, 16)])
            return s_x.count

        def attn_loads_bu(j):
            pl.step("sync", (lambda b, j: lambda e: e.dma_start(
                out=xnT[:], in_=xnt_s[b, j]))(b, j),
                waits=[(s_scw, s_scw.count), free["xnT"]], incs=[(s_xnt, 16)])
            xnt_now = s_xnt.count
            pl.step("sync", (lambda b, j: lambda e: e.dma_start(
                out=sbb[:], in_=xsb[b, :, :, :, j * TB:(j + 1) * TB].rearrange(
                    "pb c p t -> p (pb c) t")))(b, j),
                waits=[free["sbb"]], incs=[(s_sbb, 16)])
            sbb_now = s_sbb.count
            pe_bu = None
            for pb in range(2):
                for comp in range(2):
                    bank = pb * 2 + comp
                    emits = []
                    for kt in range(NHT):
                        emits.append((lambda pb, comp, kt, bank: lambda e: e.matmul(
                            ps[:, bank, :], btst[:, 0, kt, comp, pb * 128:(pb + 1) * 128],
                            xnT[:, kt, :], start=(kt == 0), stop=(kt == NHT - 1)))(pb, comp, kt, bank))
                    w = [free["bank"][bank]]
                    if pb == 0 and comp == 0:
                        w.append((s_xnt, xnt_now))
                    pe_bu = mm_group(emits, waits=w)
            return pe_bu, sbb_now

        def attn_scan(j, pe_bu):
            build_F(0)
            pow_update(0)
            dv_rot, dv_unrot = rotate_scan_unrotate(0, first_block=(j == 0),
                                                    store_rev=False, pe_bu=pe_bu,
                                                    s_free=free["ssf"])
            for k in range(4):
                free["bank"][k] = dv_rot
            return dv_unrot

        state = {}
        pe_bu0, sbb0 = attn_loads_bu(0)
        state[0] = (attn_scan(0, pe_bu0), sbb0)

        for j in range(NBLK):
            dv_unrot, sbb_now = state.pop(j)
            acc_done = {}
            x_now = load_x(j)

            last_pe = None
            for dpass in range(2):
                di = dpass
                splanes = ssf if dpass == 0 else sbb
                s_ready = dv_unrot if dpass == 0 else (s_sbb, sbb_now)
                for tsub in range(NSUB):
                    for hch in range(2):
                        bank = 2 + (tsub * 2 + hch) % 2
                        emits = []
                        for pb in range(2):
                            for comp in range(2):
                                emits.append((lambda splanes, di, pb, comp, tsub, hch, bank:
                                              lambda e: e.matmul(
                                                  ps[:, bank, :],
                                                  splanes[:, pb * 2 + comp, tsub * 128:(tsub + 1) * 128],
                                                  cmov[:, di, pb, comp, hch * 512:(hch + 1) * 512],
                                                  start=(pb == 0 and comp == 0), stop=False,
                                                  skip_group_check=True))(splanes, di, pb, comp, tsub, hch, bank))
                        for i4 in range(4):
                            ht = hch * 4 + i4
                            emits.append((lambda di, ht, tsub, i4, bank: lambda e: e.matmul(
                                ps[:, bank, i4 * 128:(i4 + 1) * 128],
                                xnT[:, ht, tsub * 128:(tsub + 1) * 128],
                                ddg[:, di, ht, :], start=False, stop=(i4 == 3),
                                skip_group_check=True))(di, ht, tsub, i4, bank))
                        w = [free["bank"][bank]]
                        if tsub == 0 and hch == 0:
                            w.append(s_ready)
                        last_pe = mm_group(emits, waits=w)
                        gs = (tsub * 2 + hch) % 2
                        act((lambda bank, gs: lambda e: e.activation(
                            out=gtmp[:, gs, :], in_=ps[:, bank, :], func=AF.Gelu))(bank, gs),
                            waits=[last_pe, (s_pl, max(0, s_pl.count - 1))])
                        ac_g = (s_ac, s_ac.count)
                        free["bank"][bank] = ac_g
                        w = [ac_g]
                        if dpass == 0 and tsub == 0 and hch == 0:
                            w.append((s_x, x_now))
                        pl.step("gpsimd", (lambda tsub, hch, gs: lambda e: e.tensor_tensor(
                            out=xbuf[:, tsub, hch * 512:(hch + 1) * 512],
                            in0=xbuf[:, tsub, hch * 512:(hch + 1) * 512],
                            in1=gtmp[:, gs, :], op=ALU.add))(tsub, hch, gs),
                            waits=w, incs=[(s_pl, 1)])
                        if dpass == 1 and hch == 1:
                            acc_done[tsub] = s_pl.count
                if dpass == 0:
                    free["ssf"] = last_pe
                else:
                    free["sbb"] = last_pe
            free["xnT"] = last_pe

            nxt = None
            if j + 1 < NBLK:
                nxt = attn_loads_bu(j + 1)

            for s in range(NSUB):
                dve((lambda s: lambda e: e.tensor_reduce(
                    out=stat[:, s:s + 1], in_=xbuf[:, s, :], axis=AX.X, op=ALU.add))(s),
                    waits=([(s_pl, acc_done[s])] if s in acc_done else []))
                dve((lambda s: lambda e: e.scalar_tensor_tensor(
                    out=junk[:], in0=xbuf[:, s, :], scalar=0.0, in1=xbuf[:, s, :],
                    op0=ALU.add, op1=ALU.mult, accum_out=stat[:, 4 + s:5 + s]))(s))
            stats_tail(stat2[:, 0:4], stat2[:, 4:8], waits=[])
            dves(lambda e: e.reciprocal(out=stat2[:, 8:12], in_=stat2[:, 4:8]))
            dves(lambda e: e.tensor_tensor(out=stat2[:, 12:16], in0=stat2[:, 0:4],
                                           in1=stat2[:, 4:8], op=ALU.mult))
            dves(lambda e: e.tensor_tensor(out=stat2[:, 16:20], in0=stat2[:, 12:16],
                                           in1=stat2[:, 12:16], op=ALU.subtract))
            dves(lambda e: e.tensor_tensor(out=stat3[:, 12:16], in0=stat2[:, 4:8],
                                           in1=stat2[:, 4:8], op=ALU.mult))
            dves(lambda e: e.tensor_tensor(out=stat3[:, 12:16], in0=stat3[:, 12:16],
                                           in1=stat[:, 8:12], op=ALU.mult))
            newton_rsqrt(stat3[:, 12:16], stat2[:, 20:24])
            dves(lambda e: e.tensor_tensor(out=stat3[:, 0:4], in0=stat2[:, 16:20],
                                           in1=stat2[:, 8:12], op=ALU.mult))
            dves(lambda e: e.tensor_tensor(out=stat3[:, 0:4], in0=stat3[:, 0:4],
                                           in1=stat2[:, 0:4], op=ALU.add))
            dves(lambda e: e.tensor_tensor(out=stat3[:, 4:8], in0=stat2[:, 4:8],
                                           in1=stat2[:, 20:24], op=ALU.mult))
            dves(lambda e: e.tensor_scalar(out=stat3[:, 8:12], in0=stat2[:, 4:8],
                                           scalar1=1.0, scalar2=None, op0=ALU.add))

            for s in range(NSUB):
                hs = s % 2
                w = []
                if s >= 2:
                    w.append((s_pe, s_pe.count))
                dve((lambda s, hs: lambda e: e.tensor_scalar(
                    out=hbuf[:, hs, :], in0=xbuf[:, s, :],
                    scalar1=stat3[:, 0 + s:1 + s], scalar2=stat3[:, 4 + s:5 + s],
                    op0=ALU.subtract, op1=ALU.mult))(s, hs), waits=w)
                dve((lambda s: lambda e: e.tensor_scalar(
                    out=xbuf[:, s, :], in0=xbuf[:, s, :],
                    scalar1=stat3[:, 8 + s:9 + s], scalar2=stat2[:, 12 + s:13 + s],
                    op0=ALU.mult, op1=ALU.subtract))(s))
                dv_h = s_dv.count
                pl.step("gpsimd", (lambda s: lambda e: e.tensor_tensor(
                    out=xbuf[:, s, :], in0=xbuf[:, s, :], in1=hb2[:], op=ALU.add))(s),
                    waits=[(s_dv, dv_h)], incs=[(s_pl, 1)])
                for g4 in range(2):
                    bank = 6 + g4
                    emits = []
                    for s4 in range(4):
                        ht_i = g4 * 4 + s4
                        emits.append((lambda hs, ht_i, bank, s4: lambda e: e.transpose(
                            ps[:, bank, s4 * 128:(s4 + 1) * 128],
                            hbuf[:, hs, ht_i * 128:(ht_i + 1) * 128], idf[:]))(hs, ht_i, bank, s4))
                    w = [free["bank"][bank]]
                    if g4 == 0:
                        w.append((s_dv, dv_h))
                    if s == 0 and g4 == 0:
                        w.append(free["hT"])
                    pe_t = mm_group(emits, waits=w)
                    act((lambda g4, s, bank: lambda e: e.activation(
                        out=hT[:, g4 * 4:(g4 + 1) * 4, s * 128:(s + 1) * 128],
                        in_=ps[:, bank, :].rearrange("p (s4 c) -> p s4 c", c=128),
                        func=AF.Copy))(g4, s, bank), waits=[pe_t])
                    free["bank"][bank] = (s_ac, s_ac.count)
            ac_hT = s_ac.count

            if nxt is not None:
                pe_bu_n, sbb_n = nxt
                state[j + 1] = (attn_scan(j + 1, pe_bu_n), sbb_n)

            w1_ready = {}

            def w1_load(ck):
                slot = ck % 2
                qsem = s_w1 if slot == 0 else s_w1b
                pl.step("sync" if slot == 0 else "gpsimd",
                        (lambda ck, slot: lambda e: e.dma_start(
                    out=w1b[:, slot, :, :, :],
                    in_=w1s[2 * ck:2 * ck + 2].rearrange("f2 p kt c -> p f2 kt c")))(ck, slot),
                    waits=[free["w1"][slot]], incs=[(qsem, 16)])
                w1_ready[ck] = (qsem, qsem.count)
            w1_load(0)
            w1_load(1)
            for ft in range(NFT):
                ck = ft // 2
                slot = ck % 2
                bank = 4 + ft % 2
                emits = []
                for kt in range(NHT):
                    emits.append((lambda slot, ft, kt, bank: lambda e: e.matmul(
                        ps[:, bank, :], w1b[:, slot, ft % 2, kt, :], hT[:, kt, :],
                        start=(kt == 0), stop=(kt == NHT - 1)))(slot, ft, kt, bank))
                w = [free["bank"][bank], w1_ready[ck]]
                if ft == 0:
                    w.append((s_ac, ac_hT))
                pe_f = mm_group(emits, waits=w)
                if ft % 2 == 1:
                    free["w1"][slot] = pe_f
                    if ck + 2 < NFT // 2:
                        w1_load(ck + 2)
                act((lambda ft, bank: lambda e: e.activation(
                    out=ybuf[:, ft, :], in_=ps[:, bank, :], func=AF.Silu,
                    bias=f1bias[:, ft:ft + 1]))(ft, bank), waits=[pe_f])
                free["bank"][bank] = (s_ac, s_ac.count)
            ac_y = s_ac.count

            pl_now = s_pl.count
            for hh in range(2):
                w2_ready = {}

                def w2_load(ck, hh=hh):
                    slot = ck % 2
                    pl.step("scalar", (lambda ck, slot, hh: lambda e: e.dma_start(
                        out=w2b[:, slot, :, 0:512],
                        in_=w2s[2 * ck:2 * ck + 2, :, hh * 512:(hh + 1) * 512].rearrange(
                            "f2 p h -> p f2 h")))(ck, slot, hh),
                        waits=[free["w2"][slot]], incs=[(s_w2, 16)])
                    w2_ready[ck] = s_w2.count
                w2_load(0)
                w2_load(1)
                pe_fc2 = None
                for ft in range(NFT):
                    ck = ft // 2
                    slot = ck % 2
                    for tsub in range(NSUB):
                        bank = 4 + tsub
                        w = []
                        if ft == 0:
                            w.append(free["bank"][bank])
                            if tsub == 0 and hh == 0:
                                w.append((s_ac, ac_y))
                        if tsub == 0:
                            w.append((s_w2, w2_ready[ck]))
                        pe_fc2 = mm_group([(lambda slot, ft, tsub, hh, bank: lambda e: e.matmul(
                            ps[:, bank, :], ybuf[:, ft, tsub * 128:(tsub + 1) * 128],
                            w2b[:, slot, ft % 2, 0:512],
                            start=(ft == 0), stop=(ft == NFT - 1),
                            skip_group_check=True))(slot, ft, tsub, hh, bank)], waits=w)
                    if ft % 2 == 1:
                        free["w2"][slot] = pe_fc2
                        if ck + 2 < NFT // 2:
                            w2_load(ck + 2)
                for tsub in range(NSUB):
                    bank = 4 + tsub
                    w = [pe_fc2]
                    if tsub == 0 and hh == 0:
                        w.append((s_pl, pl_now))
                        w.append(free["obuf"][0])
                    if hh == 1 and tsub == 0:
                        w.append((s_out, s_out.count))
                    dve((lambda tsub, hh, bank: lambda e: e.scalar_tensor_tensor(
                        out=obuf[:, tsub, :], in0=ps[:, bank, :],
                        scalar=0.5, in1=xbuf[:, tsub, hh * 512:(hh + 1) * 512],
                        op0=ALU.mult, op1=ALU.add))(tsub, hh, bank), waits=w)
                    free["bank"][bank] = (s_dv, s_dv.count)
                    spacer()
                    dv_o = s_dv.count
                    pl.step("gpsimd", (lambda: lambda e: e.tensor_tensor(
                        out=cst4[:, 8:12], in0=cst4[:, 0:4], in1=cst4[:, 4:8],
                        op=ALU.mult))(), waits=[(s_dv, dv_o)], incs=[(s_pl, 1)])
                    fence = s_pl.count
                    pl.step("gpsimd", (lambda b, j, tsub, hh: lambda e: e.dma_start(
                        out=y_out[b, j * TB + tsub * 128:j * TB + (tsub + 1) * 128,
                                  hh * 512:(hh + 1) * 512],
                        in_=obuf[:, tsub, :]))(b, j, tsub, hh),
                        waits=[(s_pl, fence)], incs=[(s_out, 16)])
            free["obuf"] = [(s_out, s_out.count)] * 2
            free["xbuf"] = (s_dv, s_dv.count)

    prep()
    free["xbuf"] = (s_dv, s_dv.count)
    for b in range(BPC):
        pass_a(b)
        pass_b(b)
    pl.step("gpsimd", lambda e: e.nop(), waits=[(s_out, s_out.count)])

    with es:
        with contextlib.ExitStack() as es2:
            for s in pl.sems:
                s.handle = es2.enter_context(nc.semaphore(s.name))
            block = es2.enter_context(nc.Block())
            for ename, attr in (("sync", "sync"), ("vector", "vector"),
                                ("scalar", "scalar"), ("tensor", "tensor"),
                                ("gpsimd", "gpsimd")):
                def make_section(ename):
                    def section(eng):
                        for (en, fn, waits, incs) in pl.steps:
                            if en != ename:
                                continue
                            for (sem, val) in waits:
                                eng.wait_ge(sem.handle, val)
                            inst = fn(eng)
                            for (sem, n) in incs:
                                inst.then_inc(sem.handle, n)
                    return section
                getattr(block, attr)(make_section(ename))
    return nc


def kernel(**inputs):
    xs = np.ascontiguousarray(inputs["L0_attn_x"])
    assert xs.shape == (16, L, H)
    bf = ml_dtypes.bfloat16
    common = {k: np.ascontiguousarray(v) for k, v in inputs.items() if k != "L0_attn_x"}
    common["c_idf"] = np.eye(128, dtype=np.float32)
    common["c_idb"] = np.eye(128).astype(bf)
    nc = build_program()
    in_maps = []
    for c in range(NCORES):
        m = dict(common)
        m["L0_attn_x"] = xs[c * BPC:(c + 1) * BPC]
        in_maps.append(m)
    res = run_bass_kernel_spmd(nc, in_maps, core_ids=list(range(NCORES)))
    return np.concatenate([r["out"] for r in res.results], axis=0)
